# version 19
# baseline (speedup 1.0000x reference)
import numpy as np
import ml_dtypes
from contextlib import ExitStack

import concourse.bass as bass
import concourse.mybir as mybir
from concourse.bass_utils import run_bass_kernel_spmd

F32 = mybir.dt.float32
BF16 = mybir.dt.bfloat16
AF = mybir.ActivationFunctionType
ALU = mybir.AluOpType

D = 1024
I = 2816
S = 4096
TT = 512
NTILE = S // TT
JT = TT // 128
KD = D // 128
KI = I // 128
NCORE = 8
NS = 11
S4 = 4

LN05 = float(np.log(0.5))
C_HALF = 0.5
C_ONE = 1.0
C_EPS_RMS = 1e-6
C_EPS_LN = 1e-5
C_INV1024 = 1.0 / 1024.0
C_NEGHALF = -0.5
C_NEG1 = -1.0
C_EPS_LOGA = 1e-8
CONSTS = [LN05, C_HALF, C_ONE, C_EPS_RMS, C_EPS_LN, C_INV1024, C_NEGHALF,
          C_NEG1, C_EPS_LOGA, -2.5, -3.5, -5.0]

DEBUG = False


class Rec:
    ENGS = ("sp", "pe", "dve", "act", "pool")

    def __init__(self):
        self.q = {e: [] for e in self.ENGS}
        self.cnt = {}

    def new_sem(self, handle):
        self.cnt[handle.name] = 0
        return handle

    def wait(self, eng, sem, val=None):
        if val is None:
            val = self.cnt[sem.name]
        if val > 0:
            self.q[eng].append(("w", sem, val))
        return val

    def op(self, eng, fn, inc=None, n=1):
        self.q[eng].append(("o", fn, inc, n))
        if inc is not None:
            self.cnt[inc.name] += n

    def val(self, sem):
        return self.cnt[sem.name]

    def replay_body(self, eng_name):
        items = self.q[eng_name]

        def body(eng):
            for it in items:
                if it[0] == "w":
                    eng.wait_ge(it[1], it[2])
                else:
                    _, fn, inc, n = it
                    ins = fn(eng)
                    if inc is not None:
                        ins.then_inc(inc, n)
        return body


def build_nc(debug=False, tap=False):
    nc = bass.Bass("TRN2", target_bir_lowering=False, debug=False,
                   detect_race_conditions=False)

    xin_d = nc.dram_tensor("x", [S, D], F32, kind="ExternalInput").ap()
    w_in_d = nc.dram_tensor("w_in", [D, D], F32, kind="ExternalInput").ap()
    w_ig_d = nc.dram_tensor("w_ig", [D, D], F32, kind="ExternalInput").ap()
    w_rg_d = nc.dram_tensor("w_rg", [D, D], F32, kind="ExternalInput").ap()
    w_out_d = nc.dram_tensor("w_out", [D, D], F32, kind="ExternalInput").ap()
    wg_d = nc.dram_tensor("wg", [I, D], F32, kind="ExternalInput").ap()
    wu_d = nc.dram_tensor("wu", [I, D], F32, kind="ExternalInput").ap()
    wd_d = nc.dram_tensor("wd", [D, I], F32, kind="ExternalInput").ap()
    big_d = nc.dram_tensor("b_ig", [D], F32, kind="ExternalInput").ap()
    brg_d = nc.dram_tensor("b_rg", [D], F32, kind="ExternalInput").ap()
    lam_d = nc.dram_tensor("lam", [D], F32, kind="ExternalInput").ap()
    nw_d = nc.dram_tensor("norm_w", [D], F32, kind="ExternalInput").ap()
    h0_d = nc.dram_tensor("h0", [D], F32, kind="ExternalInput").ap()
    ident_d = nc.dram_tensor("ident", [128, 128], BF16, kind="ExternalInput").ap()
    ones_d = nc.dram_tensor("onescol", [128, 1], BF16, kind="ExternalInput").ap()
    consts_d = nc.dram_tensor("consts", [128, len(CONSTS)], F32,
                              kind="ExternalInput").ap()
    out_d = nc.dram_tensor("out", [S, D], F32, kind="ExternalOutput").ap()

    waT_d = nc.dram_tensor("waT", [D, KD, 3, 128], BF16, kind="Internal").ap()
    woT_d = nc.dram_tensor("woT", [D, D], BF16, kind="Internal").ap()
    wgT_d = nc.dram_tensor("wgT", [D, I], BF16, kind="Internal").ap()
    wuT_d = nc.dram_tensor("wuT", [D, I], BF16, kind="Internal").ap()
    wdT_d = nc.dram_tensor("wdT", [I, D], BF16, kind="Internal").ap()

    taps = {}
    if tap:
        taps["tap_xnfm"] = nc.dram_tensor("tap_xnfm", [128, KD * TT], BF16,
                                          kind="ExternalOutput").ap()
        taps["tap_xqfm"] = nc.dram_tensor("tap_xqfm", [128, KD * TT], BF16,
                                          kind="ExternalOutput").ap()
        taps["tap_normhs"] = nc.dram_tensor("tap_normhs", [128, KD * TT], BF16,
                                            kind="ExternalOutput").ap()

    dbg = {}
    if debug:
        for name, shape, dt in [
            ("dbg_xnfm", [128, KD * TT], BF16),
            ("dbg_xqfm", [128, KD * TT], BF16),
            ("dbg_a", [128, 4 * TT], F32),
            ("dbg_normhs", [128, KD * TT], BF16),
            ("dbg_s", [128, JT * D], BF16),
            ("dbg_g", [128, 2 * D], BF16),
            ("dbg_hqfm", [128, 2 * KI * 128], BF16),
            ("dbg_rstd", [128, 8], F32),
            ("dbg_rstd2", [128, 8], F32),
            ("dbg_hst", [128, TT], F32),
            ("dbg_ut", [128, TT], F32),
            ("dbg_gat", [128, TT], F32),
            ("dbg_v4", [128, 4 * TT], BF16),
            ("dbg_carry", [128, KD], F32),
            ("dbg_waT", [D, KD * 3 * 128], BF16),
            ("dbg_wgT", [D, I], BF16),
            ("dbg_wdT", [I, D], BF16),
        ]:
            dbg[name] = nc.dram_tensor(name, shape, dt, kind="ExternalOutput").ap()

    rec = Rec()

    with ExitStack() as ctx:
        def sb(name, shape, dt):
            return ctx.enter_context(nc.sbuf_tensor(name, shape, dt))

        def psm(name, shape, dt):
            return ctx.enter_context(nc.psum_tensor(name, shape, dt))

        def sem(name):
            return rec.new_sem(ctx.enter_context(nc.semaphore(name)))

        xin = [sb(f"xin{i}", [128, D], F32) for i in range(4)]
        xn32 = sb("xn32", [128, D], F32)
        outb = xn32
        xnbf = sb("xnbf", [128, D], BF16)
        xqbf = sb("xqbf", [128, D], BF16)
        st12 = sb("st12", [128, JT * 12], F32)
        tiny = sb("tinyA", [128, 64], F32)
        QW = 1024
        q_zn = sb("q_zn", [128, QW], F32)
        q_A = sb("q_A", [128, QW], BF16)
        q_B = sb("q_B", [128, QW], BF16)
        q_C = sb("q_C", [128, QW], BF16)
        q_s5 = sb("q_s5", [128, QW], BF16)
        q_s6 = sb("q_s6", [128, QW], BF16)
        q_s7 = sb("q_s7", [128, QW], BF16)
        q_sgn = sb("q_sgn", [128, QW], BF16)
        q_sc = sb("q_sc", [128, 64], F32)
        q_inv = sb("q_inv", [128, 64], F32)
        q_hsc = sb("q_hsc", [128, 64], F32)
        xn_fm = sb("xn_fm", [128, KD * TT], BF16)
        xq_fm = sb("xq_fm", [128, KD * TT], BF16)
        a4 = sb("a4", [128, 4 * TT], F32)
        v4 = sb("v4", [128, 4 * TT], BF16)
        u_t = sb("u_t", [128, TT], F32)
        gat_t = sb("gat_t", [128, TT], F32)
        hs_t = sb("hs_t", [128, TT], F32)
        hs2_t = sb("hs2_t", [128, TT], BF16)
        normhs = sb("normhs", [128, KD * TT], BF16)
        carry = sb("carry", [128, KD], F32)
        h0c = sb("h0c", [128, KD], F32)
        wa_buf = [sb(f"wa{i}", [128, KD * 3 * 128], BF16) for i in range(2)]
        wo_buf = sb("wo_buf", [128, KD * D], BF16)
        wgu_buf = [sb(f"wgu{i}", [128, 2 * KD * 256], BF16) for i in range(2)]
        wd_buf = [sb(f"wd{i}", [128, KI * 256], BF16) for i in range(2)]
        tg_t = sb("tg_t", [128, 512], F32)
        sg_t = sb("sg_t", [128, 512], F32)
        ti_t = tg_t
        tr_t = sg_t
        h_strip = [sb(f"hstr{i}", [128, 1024], F32) for i in range(2)]
        hqbf = [sb(f"hqbf{i}", [128, I], BF16) for i in range(2)]
        hq_fm = sb("hq_fm", [128, 2 * KI * 128], BF16)
        g_j = sb("g_j", [128, 2 * D], BF16)
        s_tm = sb("s_tm", [128, JT * D], BF16)
        bih = sb("bih", [128, KD], F32)
        brh = sb("brh", [128, KD], F32)
        c4t = sb("c4t", [128, KD], F32)
        nwt = sb("nwt", [128, KD], F32)
        vtmp = sb("vtmp", [128, KD], F32)
        identt = sb("identt", [128, 128], BF16)
        onest = sb("onest", [128, 1], BF16)
        constt = sb("constt", [128, len(CONSTS)], F32)
        pp_row = [sb(f"pprow{i}", [128, D], F32) for i in range(2)]
        xrd = pp_row
        pp_q = hqbf[0]
        pp_ev = hqbf[1][:, 0:D]

        for j, cv in enumerate(CONSTS):
            nc.const_aps.register(F32, cv, constt[:, j:j + 1])

        PS = [psm(f"ps{i}", [128, 512], F32) for i in range(6)]
        PS_T = psm("ps_t", [128, 512], F32)
        PS_S = psm("ps_s", [128, 8], F32)

        sLd = sem("sLd")
        sWa = sem("sWa")
        sWo = sem("sWo")
        sWgu = sem("sWgu")
        sWd = sem("sWd")
        sX = sem("sX")
        sXr = sem("sXr")
        sSt = sem("sSt")
        sOut = sem("sOut")
        sPE = sem("sPE")
        sDV = sem("sDV")
        sAC = sem("sAC")
        sPO = sem("sPO")
        sTap = sem("sTap")

        block = ctx.enter_context(nc.Block())

        state = {"pst_free_dve": 0,
                 "pst_free_act": 0,
                 "pst_tog": 0,
                 "q_deq_done": 0,
                 "last_combine_act": 0}

        def dve_edge(fn):
            rec.op("dve", fn, inc=sDV)
            rec.wait("dve", sDV, rec.val(sDV))
            return rec.val(sDV)

        def act_edge(fn):
            rec.op("act", fn, inc=sAC)
            rec.wait("act", sAC, rec.val(sAC))
            return rec.val(sAC)

        def dma_relay(sem_, val):
            rec.wait("pool", sem_, val)
            rec.op("pool", lambda e: e.tensor_copy(
                hs2_t[:, 0:128], identt[:]))
            rec.op("pool", lambda e: e.tensor_copy(
                hs2_t[:, 128:256], identt[:]), inc=sPO)
            return rec.val(sPO)

        def quant_gen(width, nblk, src_f32, dst_bf, src_ready=None,
                      pre_wait=None, half=None):
            if half is None:
                o, ob = 0, 0
            else:
                o, ob = half * 512, half * 32
            q_zn_ = q_zn[:, o:o + 512] if half is not None else q_zn
            q_sgn_ = q_sgn[:, o:o + 512] if half is not None else q_sgn
            q_A_ = q_A[:, o:o + 512] if half is not None else q_A
            q_B_ = q_B[:, o:o + 512] if half is not None else q_B
            q_C_ = q_C[:, o:o + 512] if half is not None else q_C
            q_s5_ = q_s5[:, o:o + 512] if half is not None else q_s5
            q_s6_ = q_s6[:, o:o + 512] if half is not None else q_s6
            q_s7_ = q_s7[:, o:o + 512] if half is not None else q_s7
            q_sc_ = q_sc[:, ob:ob + 32] if half is not None else q_sc
            q_inv_ = q_inv[:, ob:ob + 32] if half is not None else q_inv
            q_hsc_ = q_hsc[:, ob:ob + 32] if half is not None else q_hsc
            if pre_wait is not None:
                pre_wait()
            rec.wait("act", sDV, rec.val(sDV))
            if src_ready is not None:
                rec.wait("act", src_ready[0], src_ready[1])
            rec.op("act", lambda e, s=src_f32, w=width: e.sign(
                q_sgn_[:, :w], s), inc=sAC)
            yield
            dve_edge(lambda e, s=src_f32, w=width, nb=nblk: e.tensor_reduce(
                q_sc_[:, :nb].unsqueeze(2),
                s.rearrange("p (b s) -> p b s", s=16),
                mybir.AxisListType.X, ALU.max, apply_absolute_value=True))
            yield
            dve_edge(lambda e, nb=nblk: e.tensor_scalar(
                q_sc_[:, :nb], q_sc_[:, :nb], 1.0 / 6.0, 1e-12, ALU.mult, ALU.max))
            dve_edge(lambda e, nb=nblk: e.reciprocal(
                q_inv_[:, :nb], q_sc_[:, :nb]))
            dve_edge(lambda e, nb=nblk: e.tensor_scalar(
                q_hsc_[:, :nb], q_sc_[:, :nb], 0.5, None, ALU.mult))
            yield
            rec.op("dve", lambda e, s=src_f32, w=width, nb=nblk: e.tensor_tensor(
                q_zn_[:, :w].rearrange("p (b s) -> p b s", s=16),
                s.rearrange("p (b s) -> p b s", s=16),
                q_inv_[:, :nb].unsqueeze(2).broadcast_to([128, nb, 16]),
                ALU.mult), inc=sDV)
            zn_done = rec.val(sDV)
            rec.wait("act", sDV, zn_done)
            rec.op("act", lambda e, w=width: e.activation(
                q_zn_[:, :w], q_zn_[:, :w], AF.Abs), inc=sAC)
            abs_done = rec.val(sAC)
            rec.wait("dve", sAC, abs_done)
            yield
            rec.wait("act", sAC, abs_done)
            rec.op("act", lambda e, w=width: e.sign(q_s5_[:, :w], q_zn_[:, :w],
                                                    bias=-2.5))
            yield
            rec.op("act", lambda e, w=width: e.sign(q_s6_[:, :w], q_zn_[:, :w],
                                                    bias=-3.5))
            yield
            rec.op("act", lambda e, w=width: e.sign(q_s7_[:, :w], q_zn_[:, :w],
                                                    bias=-5.0), inc=sAC)
            s_done = rec.val(sAC)
            yield
            rec.wait("dve", sAC, abs_done)
            dve_edge(lambda e, w=width: e.tensor_scalar(
                q_A_[:, :w], q_zn_[:, :w], 0.25, 4.0, ALU.is_gt, ALU.add))
            yield
            dve_edge(lambda e, w=width: e.tensor_scalar(
                q_B_[:, :w], q_zn_[:, :w], 0.75, None, ALU.is_gt))
            yield
            dve_edge(lambda e, w=width: e.tensor_tensor(
                q_C_[:, :w], q_A_[:, :w], q_B_[:, :w], ALU.add))
            yield
            dve_edge(lambda e, w=width: e.tensor_scalar(
                q_A_[:, :w], q_zn_[:, :w], 1.25, None, ALU.is_gt))
            yield
            dve_edge(lambda e, w=width: e.tensor_scalar(
                q_B_[:, :w], q_zn_[:, :w], 1.75, None, ALU.is_gt))
            yield
            dve_edge(lambda e, w=width: e.tensor_tensor(
                q_A_[:, :w], q_A_[:, :w], q_B_[:, :w], ALU.add))
            yield
            dve_edge(lambda e, w=width: e.tensor_tensor(
                q_C_[:, :w], q_C_[:, :w], q_A_[:, :w], ALU.add))
            yield
            rec.wait("dve", sAC, s_done)
            dve_edge(lambda e, w=width: e.tensor_tensor(
                q_A_[:, :w], q_s5_[:, :w], q_s6_[:, :w], ALU.add))
            yield
            dve_edge(lambda e, w=width: e.tensor_tensor(
                q_C_[:, :w], q_C_[:, :w], q_A_[:, :w], ALU.add))
            yield
            dve_edge(lambda e, w=width: e.scalar_tensor_tensor(
                q_A_[:, :w], q_s7_[:, :w], 2.0, q_C_[:, :w], ALU.mult, ALU.add))
            yield
            dve_edge(lambda e, w=width: e.tensor_tensor(
                q_B_[:, :w], q_A_[:, :w], q_sgn_[:, :w], ALU.mult))
            yield
            rec.op("dve", lambda e, d=dst_bf, w=width, nb=nblk: e.tensor_tensor(
                d.rearrange("p (b s) -> p b s", s=16),
                q_B_[:, :w].rearrange("p (b s) -> p b s", s=16),
                q_hsc_[:, :nb].unsqueeze(2).broadcast_to([128, nb, 16]),
                ALU.mult), inc=sDV)
            state["q_deq_done"] = rec.val(sDV)

        def quant(width, nblk, src_f32, dst_bf, src_ready=None,
                  pre_wait_=None):
            for _ in quant_gen(width, nblk, src_f32, dst_bf, src_ready,
                               pre_wait=pre_wait_):
                pass
            return rec.val(sDV)

        def quant2(width, nblk, src_f32, dst_bf, src_ready=None):
            h = (nblk // 2) * 16
            gens = [quant_gen(h, nblk // 2, src_f32[:, 0:h],
                              dst_bf[:, 0:h], src_ready, half=0),
                    quant_gen(width - h, nblk - nblk // 2, src_f32[:, h:width],
                              dst_bf[:, h:width], src_ready, half=1)]
            for _ in range(5):
                next(gens[0])
                next(gens[1])
            src_mark = rec.val(sDV)
            live = [True, True]
            while any(live):
                for i in (0, 1):
                    if live[i]:
                        try:
                            next(gens[i])
                        except StopIteration:
                            live[i] = False
            return src_mark

        tp_free = {}

        def transpose_chunks(srcs, mk_out, pre_evac_wait=None, alt=None):
            banks = [PS_T] if alt is None else [PS_T, alt]
            i = 0
            p = 0
            while i < len(srcs):
                cl = min(4, len(srcs) - i)
                bank = banks[p % len(banks)]
                fd, fa = tp_free.get(id(bank), (0, 0))
                rec.wait("pe", sDV, fd)
                rec.wait("pe", sAC, fa)
                for c in range(cl):
                    rec.op("pe", lambda e, s=srcs[i + c], o=c * 128, b=bank:
                           e.matmul(b[:, o:o + 128], s, identt[:], start=True,
                                    stop=True), inc=sPE if c == cl - 1 else None)
                pe_c = rec.val(sPE)
                out_ap = mk_out(i, cl)
                src_ap = bank[:, :cl * 128]
                if out_ap.ndim == 3:
                    src_ap = src_ap.rearrange("p (k c) -> p k c", c=128)
                eng = "dve" if state["pst_tog"] == 0 else "act"
                state["pst_tog"] ^= 1
                rec.wait(eng, sPE, pe_c)
                if pre_evac_wait is not None:
                    rec.wait(eng, pre_evac_wait[0], pre_evac_wait[1])
                if eng == "dve":
                    rec.op("dve", lambda e, oa=out_ap, sa=src_ap: e.tensor_copy(
                        oa, sa), inc=sDV)
                    tp_free[id(bank)] = (rec.val(sDV), tp_free.get(id(bank), (0, 0))[1])
                else:
                    rec.op("act", lambda e, oa=out_ap, sa=src_ap: e.activation(
                        oa, sa, AF.Copy), inc=sAC)
                    tp_free[id(bank)] = (tp_free.get(id(bank), (0, 0))[0], rec.val(sAC))
                i += cl
                p += 1
            return rec.val(sPE), rec.val(sDV), rec.val(sAC)

        def ln_stats(src, jcol):
            rec.op("dve", lambda e, s=src, j=jcol: e.bn_stats(
                st12[:, j * 12:j * 12 + 6], s[:, 0:512]), inc=sDV)
            dve_edge(lambda e, s=src, j=jcol: e.bn_stats(
                st12[:, j * 12 + 6:j * 12 + 12], s[:, 512:1024]))
            return rec.val(sDV)

        def ln_combine():
            rec.wait("dve", sAC, state["last_combine_act"])
            v12 = st12[:].rearrange("p (j s) -> p j s", s=12)
            dve_edge(lambda e: e.tensor_reduce(
                tiny[:, 0:4].unsqueeze(2), v12[:, :, 1:12:3],
                mybir.AxisListType.X, ALU.add))
            dve_edge(lambda e: e.tensor_scalar(
                tiny[:, 0:4], tiny[:, 0:4], 0.25, None, ALU.mult))
            dve_edge(lambda e: e.tensor_reduce(
                tiny[:, 24:28].unsqueeze(2), v12[:, :, 2:12:3],
                mybir.AxisListType.X, ALU.add))
            dve_edge(lambda e: e.tensor_tensor(
                tiny[:, 44:60].rearrange("p (j s) -> p j s", s=4),
                v12[:, :, 1:12:3], v12[:, :, 1:12:3], ALU.mult))
            dve_edge(lambda e: e.tensor_reduce(
                tiny[:, 32:36].unsqueeze(2),
                tiny[:, 44:60].rearrange("p (j s) -> p j s", s=4),
                mybir.AxisListType.X, ALU.add))
            dve_edge(lambda e: e.tensor_scalar(
                tiny[:, 24:28], tiny[:, 24:28], C_INV1024, None, ALU.mult))
            dve_edge(lambda e: e.scalar_tensor_tensor(
                tiny[:, 32:36], tiny[:, 32:36], 0.25, tiny[:, 24:28],
                ALU.mult, ALU.add))
            dve_edge(lambda e: e.tensor_tensor(
                tiny[:, 28:32], tiny[:, 0:4], tiny[:, 0:4], ALU.mult))
            dve_edge(lambda e: e.tensor_tensor(
                tiny[:, 24:28], tiny[:, 32:36], tiny[:, 28:32], ALU.subtract))
            var_done = rec.val(sDV)
            rec.wait("act", sDV, var_done)
            act_edge(lambda e: e.activation(
                tiny[:, 36:40], tiny[:, 24:28], AF.Ln, bias=C_EPS_LN))
            rec.op("act", lambda e: e.activation(
                tiny[:, 4:8], tiny[:, 36:40], AF.Exp, scale=C_NEGHALF),
                inc=sAC)
            state["last_combine_act"] = rec.val(sAC)
            return rec.val(sAC)

        rec.op("sp", lambda e: e.dma_start(out=constt[:], in_=consts_d), inc=sLd, n=16)
        rec.op("sp", lambda e: e.dma_start(out=identt[:], in_=ident_d), inc=sLd, n=16)
        rec.op("sp", lambda e: e.dma_start(out=onest[:], in_=ones_d), inc=sLd, n=16)
        for dst, src in [(bih, big_d), (brh, brg_d), (vtmp, lam_d),
                         (nwt, nw_d), (h0c, h0_d)]:
            rec.op("sp", lambda e, d=dst, s=src: e.dma_start(
                out=d[:], in_=s.rearrange("(m p) -> p m", p=128),
                allow_slow_non_contiguous=True), inc=sLd, n=16)
        xload_done = [0] * JT
        for j in range(JT):
            rec.op("sp", lambda e, j=j: e.dma_start(
                out=xin[j][:], in_=xin_d[j * 128:(j + 1) * 128, :]),
                inc=sX, n=16)
            xload_done[j] = rec.val(sX)
        vec_ld = rec.val(sLd)
        vec_rdy = dma_relay(sLd, vec_ld)
        rec.wait("dve", sPO, vec_rdy)
        dve_edge(lambda e: e.tensor_scalar(bih[:], bih[:], 0.5, None, ALU.mult))
        dve_edge(lambda e: e.tensor_scalar(brh[:], brh[:], 0.5, None, ALU.mult))
        rec.op("dve", lambda e: e.tensor_copy(carry[:], h0c[:]), inc=sDV)
        rec.wait("act", sDV, rec.val(sDV))
        rec.op("act", lambda e: e.activation(c4t[:], vtmp[:], AF.Tanh,
                                             scale=C_HALF), inc=sAC)
        rec.wait("dve", sAC, rec.val(sAC))
        rec.op("dve", lambda e: e.tensor_scalar(c4t[:], c4t[:], 0.5, 0.5,
                                                ALU.mult, ALU.add), inc=sDV)
        rec.wait("act", sDV, rec.val(sDV))
        rec.op("act", lambda e: e.activation(c4t[:], c4t[:], AF.Ln,
                                             bias=C_EPS_LOGA), inc=sAC)
        rec.wait("dve", sAC, rec.val(sAC))
        rec.op("dve", lambda e: e.tensor_scalar(c4t[:], c4t[:], 4.0, None,
                                                ALU.mult), inc=sDV)

        def pp_attn_weight(w_dram, pslot, halve=False):
            rec.wait("dve", sPE, rec.val(sPE))
            for r in range(KD):
                q_ = "sp" if r % 2 == 0 else "act"
                rec.wait(q_, sDV, rec.val(sDV))
                rec.op(q_, lambda e, r=r, w=w_dram: e.dma_start(
                    out=pp_row[r % 2][:, :D], in_=w[r * 128:(r + 1) * 128, :]),
                    inc=sLd, n=16)
                rec.wait("dve", sPO, dma_relay(sLd, rec.val(sLd)))
                if halve:
                    rec.op("dve", lambda e, r=r: e.tensor_scalar(
                        wo_buf[:, r * D:(r + 1) * D], pp_row[r % 2][:, :D],
                        0.5, None, ALU.mult), inc=sDV)
                else:
                    rec.op("dve", lambda e, r=r: e.tensor_copy(
                        wo_buf[:, r * D:(r + 1) * D], pp_row[r % 2][:, :D]),
                        inc=sDV)
            cast_done = rec.val(sDV)
            for k in range(KD):
                rec.wait("pe", sDV, cast_done)
                st_now = rec.val(sSt)
                _, dv, ac = transpose_chunks(
                    [wo_buf[:, r * D + k * 128: r * D + (k + 1) * 128]
                     for r in range(KD)],
                    lambda i0, cl: pp_ev[:, i0 * 128:(i0 + cl) * 128],
                    pre_evac_wait=(sSt, st_now), alt=PS[5])
                rec.wait("sp", sDV, dv)
                rec.wait("sp", sAC, ac)
                if pslot >= 0:
                    outap = waT_d[k * 128:(k + 1) * 128, :, pslot, :]
                    inap = pp_ev.rearrange("p (r c) -> p r c", c=128)
                else:
                    outap = woT_d[k * 128:(k + 1) * 128, :]
                    inap = pp_ev
                rec.op("sp", lambda e, o=outap, i=inap: e.dma_start(
                    out=o, in_=i), inc=sSt, n=16)

        pp_attn_weight(w_in_d, 0, halve=True)
        pp_attn_weight(w_ig_d, 1)
        pp_attn_weight(w_rg_d, 2)
        pp_attn_weight(w_out_d, -1)
        pp_st_attn = rec.val(sSt)

        pp_src_done = [0, 0]

        def pp_ffn_gu(w_dram, scratch):
            for r in range(KI):
                q_ = "sp" if r % 2 == 0 else "act"
                rec.wait(q_, sDV, pp_src_done[r % 2])
                rec.op(q_, lambda e, r=r, w=w_dram: e.dma_start(
                    out=pp_row[r % 2][:, :D], in_=w[r * 128:(r + 1) * 128, :]),
                    inc=sLd, n=16)
                row_rdy = dma_relay(sLd, rec.val(sLd))
                rec.wait("dve", sPO, row_rdy)
                rec.wait("dve", sPE, rec.val(sPE))
                pp_src_done[r % 2] = quant2(
                    D, 64, pp_row[r % 2][:, :D], pp_q[:, :D],
                    src_ready=(sPO, row_rdy))
                qd = rec.val(sDV)
                rec.wait("pe", sDV, qd)
                st_now = rec.val(sSt)
                _, dv, ac = transpose_chunks(
                    [pp_q[:, k * 128:(k + 1) * 128] for k in range(KD)],
                    lambda i0, cl: pp_ev[:, i0 * 128:(i0 + cl) * 128],
                    pre_evac_wait=(sSt, st_now), alt=PS[5])
                rec.wait("sp", sDV, dv)
                rec.wait("sp", sAC, ac)
                rec.op("sp", lambda e, r=r, sc=scratch: e.dma_start(
                    out=sc.rearrange("(k p) i -> p k i", p=128)[
                        :, :, r * 128:(r + 1) * 128],
                    in_=pp_ev.rearrange("p (k c) -> p k c", c=128)),
                    inc=sSt, n=16)

        pp_ffn_gu(wg_d, wgT_d)
        pp_ffn_gu(wu_d, wuT_d)
        pp_st_gu = rec.val(sSt)

        for r in range(KD):
            rec.wait("dve", sPE, rec.val(sPE))
            for t3, (off, w, nb) in enumerate(
                    ((0, 960, 60), (960, 960, 60), (1920, 896, 56))):
                q_ = "sp" if t3 % 2 == 0 else "act"
                rec.wait(q_, sDV, pp_src_done[t3 % 2])
                rec.op(q_, lambda e, r=r, off=off, w=w, t3=t3: e.dma_start(
                    out=pp_row[t3 % 2][:, :w],
                    in_=wd_d[r * 128:(r + 1) * 128, off:off + w]),
                    inc=sLd, n=16)
                row_rdy = dma_relay(sLd, rec.val(sLd))
                rec.wait("dve", sPO, row_rdy)
                pp_src_done[t3 % 2] = quant2(
                    w, nb, pp_row[t3 % 2][:, :w],
                    pp_q[:, off:off + w], src_ready=(sPO, row_rdy))
            qd = rec.val(sDV)
            for kg, nk in ((0, 8), (1, 8), (2, 6)):
                rec.wait("pe", sDV, qd)
                st_now = rec.val(sSt)
                _, dv, ac = transpose_chunks(
                    [pp_q[:, (kg * 8 + kk) * 128:(kg * 8 + kk + 1) * 128]
                     for kk in range(nk)],
                    lambda i0, cl: pp_ev[:, i0 * 128:(i0 + cl) * 128],
                    pre_evac_wait=(sSt, st_now), alt=PS[5])
                rec.wait("sp", sDV, dv)
                rec.wait("sp", sAC, ac)
                rec.op("sp", lambda e, r=r, kg=kg, nk=nk: e.dma_start(
                    out=wdT_d.rearrange("(k p) d -> p k d", p=128)[
                        :, kg * 8:kg * 8 + nk, r * 128:(r + 1) * 128],
                    in_=pp_ev[:, :nk * 128].rearrange("p (k c) -> p k c", c=128)),
                    inc=sSt, n=16)

        pp_store_all = rec.val(sSt)
        po_base = rec.val(sPO)

        for j in range(JT):
            rec.wait("dve", sX, xload_done[j])
            ln_stats(xin[j], j)
        rstd_next_done = ln_combine()

        xrd_used = [rec.val(sDV), rec.val(sDV)]
        outb_used = 0
        woT_loaded = 0
        carr_done = 0
        prev_pe_end = 0
        prev_dve_end = 0
        prev_act_end = 0
        prev_po_end = 0

        for T in range(NTILE):
            tok0 = T * TT
            rec.wait("pe", sDV, prev_dve_end)
            rec.wait("pe", sAC, prev_act_end)
            rec.wait("dve", sPE, prev_pe_end)
            rec.wait("dve", sAC, prev_act_end)
            rec.wait("act", sPE, prev_pe_end)
            rec.wait("act", sDV, prev_dve_end)
            rec.wait("pool", sDV, prev_dve_end)
            rec.wait("dve", sOut, rec.val(sOut))
            if tap and T == 1:
                rec.wait("dve", sTap, rec.val(sTap))
                rec.wait("act", sTap, rec.val(sTap))

            xrd_ld = [0, 0]
            for jj in range(2):
                rec.wait("sp", sDV, xrd_used[jj])
                rec.op("sp", lambda e, jj=jj, t0=tok0: e.dma_start(
                    out=xrd[jj][:],
                    in_=xin_d[t0 + jj * 128:t0 + (jj + 1) * 128, :]),
                    inc=sXr, n=16)
                xrd_ld[jj] = rec.val(sXr)

            if T == 0:
                rec.wait("sp", sSt, pp_store_all)
                rec.op("sp", lambda e: e.dma_start(
                    out=wo_buf[:].rearrange("p (k h) -> p k h", h=D),
                    in_=woT_d.rearrange("(k p) h -> p k h", p=128)),
                    inc=sWo, n=16)
                woT_loaded = rec.val(sWo)
            rstd_this = rstd_next_done
            pe_xn_last = 0
            pe_xq_last = 0
            for j in range(JT):
                rec.wait("dve", sX, xload_done[j])
                rec.wait("dve", sAC, rstd_this)
                dve_edge(lambda e, j=j: e.tensor_scalar(
                    xn32[:], xin[j][:], tiny[:, j:j + 1],
                    tiny[:, 4 + j:5 + j], ALU.subtract, ALU.mult))
                if T + 1 < NTILE:
                    rec.wait("sp", sDV, rec.val(sDV))
                    rec.op("sp", lambda e, j=j, nt0=tok0 + TT: e.dma_start(
                        out=xin[j][:],
                        in_=xin_d[nt0 + j * 128:nt0 + (j + 1) * 128, :]),
                        inc=sX, n=16)
                    xload_done[j] = rec.val(sX)
                rec.wait("dve", sPE, pe_xn_last)
                rec.op("dve", lambda e: e.tensor_copy(xnbf[:], xn32[:]), inc=sDV)
                xnbf_done = rec.val(sDV)
                rec.wait("dve", sPE, pe_xq_last)
                quant2(D, 64, xn32[:], xqbf[:])
                xq_done = rec.val(sDV)
                rec.wait("pe", sDV, xnbf_done)
                pe_xn_last, _, _ = transpose_chunks(
                    [xnbf[:, k * 128:(k + 1) * 128] for k in range(KD)],
                    lambda i0, cl, j=j: xn_fm[:].rearrange(
                        "p (k t) -> p k t", t=TT)
                    [:, i0:i0 + cl, j * 128:(j + 1) * 128], alt=PS[5])
                rec.wait("pe", sDV, xq_done)
                pe_xq_last, _, _ = transpose_chunks(
                    [xqbf[:, k * 128:(k + 1) * 128] for k in range(KD)],
                    lambda i0, cl, j=j: xq_fm[:].rearrange(
                        "p (k t) -> p k t", t=TT)
                    [:, i0:i0 + cl, j * 128:(j + 1) * 128], alt=PS[5])
            xnfm_all = (rec.val(sDV), rec.val(sAC))
            xqfm_all = (rec.val(sDV), rec.val(sAC))
            if tap and T == 0:
                rec.wait("sp", sDV, xnfm_all[0])
                rec.wait("sp", sAC, xnfm_all[1])
                rec.op("sp", lambda e: e.dma_start(
                    out=taps["tap_xnfm"], in_=xn_fm[:]), inc=sTap, n=16)
                rec.op("sp", lambda e: e.dma_start(
                    out=taps["tap_xqfm"], in_=xq_fm[:]), inc=sTap, n=16)

            wa_ld = [0] * KD

            def emit_wa(m2, guard, T=T):
                if m2 >= KD:
                    return
                if T == 0:
                    rec.wait("pool", sSt, pp_st_attn)
                rec.wait("pool", sPE, guard)
                rec.op("pool", lambda e, m=m2, par=m2 % 2: e.dma_start(
                    out=wa_buf[par][:],
                    in_=waT_d.rearrange("(k p) m q c -> p k m q c", p=128)
                    [:, :, m, :, :]), inc=sWa, n=16)
                wa_ld[m2] = rec.val(sWa)

            emit_wa(0, prev_pe_end)
            emit_wa(1, prev_pe_end)
            proj_done_m = []
            scanned = [0] * KD
            hs2_done_m = [0] * KD
            mss_pe = [0] * KD
            for m in range(KD):
                bank = (m % 2) * 3
                par = m % 2
                mg = m % 4
                rec.wait("pe", sWa, wa_ld[m])
                rec.wait("pe", sDV, xnfm_all[0])
                rec.wait("pe", sAC, xnfm_all[1])
                if m >= 2:
                    rec.wait("pe", sAC, proj_done_m[m - 2][0])
                    rec.wait("pe", sDV, proj_done_m[m - 2][1])
                for q in range(3):
                    for k in range(KD):
                        rec.op("pe", lambda e, q=q, k=k, bank=bank, par=par: e.matmul(
                            PS[bank + q][:],
                            wa_buf[par][:].rearrange(
                                "p (k q c) -> p k q c", k=KD, q=3)[:, k, q, :],
                            xn_fm[:, k * TT:(k + 1) * TT],
                            start=(k == 0), stop=(k == KD - 1)),
                            inc=sPE if (q == 2 and k == KD - 1) else None)
                pe_m = rec.val(sPE)
                rec.wait("act", sPE, pe_m)
                if m >= 1:
                    rec.wait("act", sDV, proj_done_m[m - 1][1])
                if m >= 4:
                    rec.wait("act", sDV, scanned[m - 4])
                rec.op("act", lambda e, m=m, bank=bank: e.activation(
                    ti_t[:], PS[bank + 1][:], AF.Tanh, bias=bih[:, m:m + 1],
                    scale=C_HALF))
                act_edge(lambda e, m=m, bank=bank: e.activation(
                    tr_t[:], PS[bank + 2][:], AF.Tanh, bias=brh[:, m:m + 1],
                    scale=C_HALF))
                rec.op("act", lambda e, m=m, mg=mg: e.activation(
                    a4[:, mg * TT:(mg + 1) * TT], tr_t[:], AF.Exp,
                    bias=c4t[:, m:m + 1], scale=c4t[:, m:m + 1]), inc=sAC)
                act_m = rec.val(sAC)
                emit_wa(m + 2, pe_m)
                rec.wait("dve", sAC, act_m)
                if m >= 4:
                    rec.wait("dve", sAC, hs2_done_m[m - 4])
                dve_edge(lambda e, bank=bank, mg=mg: e.scalar_tensor_tensor(
                    v4[:, mg * TT:(mg + 1) * TT], ti_t[:], 1.0, PS[bank + 0][:],
                    ALU.add, ALU.mult))
                dve_m = rec.val(sDV)
                proj_done_m.append((act_m, dve_m))

                if mg == 3:
                    for mm in range(m - 3, m + 1):
                        mgg = mm % 4
                        if mm >= 1:
                            rec.wait("act", sDV, scanned[mm - 1])
                            rec.wait("act", sPE, mss_pe[mm - 1])
                        act_edge(lambda e, mgg=mgg: e.activation(
                            u_t[:], a4[:, mgg * TT:(mgg + 1) * TT], AF.Square))
                        act_edge(lambda e: e.activation(
                            u_t[:], u_t[:], AF.Copy, bias=C_ONE, scale=C_NEG1))
                        act_edge(lambda e: e.activation(
                            u_t[:], u_t[:], AF.Ln, bias=C_EPS_RMS))
                        rec.op("act", lambda e: e.activation(
                            u_t[:], u_t[:], AF.Exp, scale=C_HALF),
                            inc=sAC)
                        u_done = rec.val(sAC)
                        rec.wait("dve", sAC, u_done)
                        dve_edge(lambda e, mgg=mgg: e.tensor_tensor(
                            gat_t[:], u_t[:], v4[:, mgg * TT:(mgg + 1) * TT],
                            ALU.mult))
                        if mm >= 1:
                            rec.wait("dve", sAC, hs2_done_m[mm - 1])
                            rec.wait("dve", sPO, po_base + T * KD + mm)
                        elif T > 0:
                            rec.wait("dve", sPO, po_base + T * KD)
                        rec.op("dve", lambda e, mm=mm: e.tensor_tensor_scan(
                            hs_t[:], a4[:, (mm % 4) * TT:((mm % 4) + 1) * TT],
                            gat_t[:], carry[:, mm:mm + 1], ALU.mult, ALU.add),
                            inc=sDV)
                        scanned[mm] = rec.val(sDV)
                        rec.wait("pool", sDV, scanned[mm])
                        rec.op("pool", lambda e, mm=mm: e.tensor_copy(
                            carry[:, mm:mm + 1], hs_t[:, TT - 1:TT]), inc=sPO)
                        rec.wait("act", sDV, scanned[mm])
                        rec.op("act", lambda e: e.activation(
                            hs2_t[:], hs_t[:], AF.Square), inc=sAC)
                        hs2_done_m[mm] = rec.val(sAC)
                        rec.wait("dve", sDV, scanned[mm])
                        rec.op("dve", lambda e, mm=mm: e.tensor_scalar(
                            normhs[:, mm * TT:(mm + 1) * TT], hs_t[:],
                            nwt[:, mm:mm + 1], None, ALU.mult), inc=sDV)
                        rec.wait("pe", sAC, hs2_done_m[mm])
                        for j2 in range(JT):
                            rec.op("pe", lambda e, j2=j2, mm=mm: e.matmul(
                                PS_S[:, j2:j2 + 1],
                                hs2_t[:, j2 * 128:(j2 + 1) * 128], onest[:],
                                start=(mm == 0 and j2 == 0),
                                stop=(mm == KD - 1),
                                skip_group_check=True),
                                inc=sPE if j2 == JT - 1 else None)
                        mss_pe[mm] = rec.val(sPE)
            carr_all = rec.val(sPO)
            normhs_done = rec.val(sDV)
            mss_done = rec.val(sPE)
            if tap and T == 0:
                rec.wait("sp", sDV, normhs_done)
                rec.op("sp", lambda e: e.dma_start(
                    out=taps["tap_normhs"], in_=normhs[:]), inc=sTap, n=16)

            rec.wait("dve", sPE, mss_done)
            dve_edge(lambda e: e.tensor_copy(tiny[:, 12:16], PS_S[:, 0:4]))
            rec.wait("act", sDV, rec.val(sDV))
            act_edge(lambda e: e.activation(
                tiny[:, 40:44], tiny[:, 12:16], AF.Ln, bias=C_EPS_RMS,
                scale=C_INV1024))
            rec.op("act", lambda e: e.activation(
                tiny[:, 16:20], tiny[:, 40:44], AF.Exp,
                scale=C_NEGHALF), inc=sAC)
            rstd_rms_done = rec.val(sAC)

            s_parts = []
            s_bank_free = {}
            for j in range(JT):
                for dch in range(2):
                    bank = 0 if (j * 2 + dch) % 2 == 0 else 3
                    rec.wait("pe", sWo, woT_loaded)
                    rec.wait("pe", sDV, normhs_done)
                    if len(s_parts) >= 2:
                        rec.wait("pe", sDV, s_parts[-2])
                    for k in range(KD):
                        rec.op("pe", lambda e, j=j, dch=dch, k=k, bank=bank: e.matmul(
                            PS[bank][:],
                            normhs[:, k * TT + j * 128: k * TT + (j + 1) * 128],
                            wo_buf[:, k * D + dch * 512: k * D + (dch + 1) * 512],
                            start=(k == 0), stop=(k == KD - 1)),
                            inc=sPE if k == KD - 1 else None)
                    pe_s = rec.val(sPE)
                    rec.wait("dve", sPE, pe_s)
                    rec.wait("dve", sAC, rstd_rms_done)
                    rec.op("dve", lambda e, j=j, dch=dch, bank=bank: e.tensor_scalar(
                        s_tm[:, j * D + dch * 512: j * D + (dch + 1) * 512],
                        PS[bank][:], tiny[:, 16 + j:17 + j], None, ALU.mult),
                        inc=sDV)
                    s_parts.append(rec.val(sDV))
                    s_bank_free[bank] = (0, rec.val(sDV))
            s_done = s_parts[-1]

            if T + 1 < NTILE:
                for j in range(JT):
                    rec.wait("dve", sX, xload_done[j])
                    ln_stats(xin[j], j)
                rstd_next_done = ln_combine()

            g_done = 0
            prev_jp_gu_pe = 0
            pend_q = []

            def drain_q(k=None):
                n = 0
                while pend_q and (k is None or n < k):
                    try:
                        next(pend_q[0])
                    except StopIteration:
                        pend_q.pop(0)
                    n += 1

            bank_free = {b: (proj_done_m[-1][0],
                             max(s_done, s_bank_free.get(b, (0, 0))[1]))
                         for b in range(6)}
            hq_tr_pe = 0
            down_pe_last = 0
            for jp in range(2):
                wgu_ld = {}

                def emit_wgu(sl2, guard, T=T, jp=jp):
                    if sl2 >= NS:
                        return
                    if T == 0 and jp == 0:
                        rec.wait("pool", sSt, pp_st_gu)
                    rec.wait("pool", sPE, guard)
                    rec.op("pool", lambda e, sl=sl2, par=sl2 % 2: e.dma_start(
                        out=wgu_buf[par][:].rearrange(
                            "p (k c) -> p k c", c=512)[:, :, 0:256],
                        in_=wgT_d.rearrange("(k p) i -> p k i", p=128)
                        [:, :, sl * 256:(sl + 1) * 256]), inc=sWgu, n=16)
                    rec.op("pool", lambda e, sl=sl2, par=sl2 % 2: e.dma_start(
                        out=wgu_buf[par][:].rearrange(
                            "p (k c) -> p k c", c=512)[:, :, 256:512],
                        in_=wuT_d.rearrange("(k p) i -> p k i", p=128)
                        [:, :, sl * 256:(sl + 1) * 256]), inc=sWgu, n=16)
                    wgu_ld[sl2] = rec.val(sWgu)

                gu_guard0 = prev_jp_gu_pe if jp == 1 else prev_pe_end
                emit_wgu(0, gu_guard0)
                emit_wgu(1, gu_guard0)
                for sl in range(NS):
                    par = sl % 2
                    for jj in range(2):
                        j = jp * 2 + jj
                        bank = jj * 3 + sl % 2
                        rec.wait("pe", sWgu, wgu_ld[sl])
                        rec.wait("pe", sDV, xqfm_all[0])
                        rec.wait("pe", sAC, xqfm_all[1])
                        rec.wait("pe", sAC, bank_free[bank][0])
                        rec.wait("pe", sDV, bank_free[bank][1])
                        for k in range(KD):
                            rec.op("pe", lambda e, j=j, sl=sl, k=k,
                                   bank=bank, par=par: e.matmul(
                                PS[bank][:],
                                xq_fm[:, k * TT + j * 128: k * TT + (j + 1) * 128],
                                wgu_buf[par][:].rearrange(
                                    "p (k c) -> p k c", c=512)[:, k, :],
                                start=(k == 0), stop=(k == KD - 1)),
                                inc=sPE if k == KD - 1 else None)
                        pe_gu = rec.val(sPE)
                        rec.wait("act", sPE, pe_gu)
                        rec.wait("act", sDV, rec.val(sDV))
                        rec.op("act", lambda e, bank=bank: e.activation(
                            sg_t[:, 0:256], PS[bank][:, 0:256], AF.Silu),
                            inc=sAC)
                        tg_done = rec.val(sAC)
                        if jj == 1:
                            emit_wgu(sl + 2, pe_gu)
                        rec.wait("dve", sAC, tg_done)
                        dve_edge(lambda e, jj=jj, sl=sl, bank=bank: e.tensor_tensor(
                            h_strip[jj][:, (sl % 4) * 256:(sl % 4 + 1) * 256],
                            sg_t[:, 0:256], PS[bank][:, 256:512], ALU.mult))
                        bank_free[bank] = (rec.val(sAC), rec.val(sDV))
                    if sl in (3, 7, 10):
                        strip = sl // 4
                        off = strip * 1024
                        w = 1024 if sl < 10 else 768
                        nb = w // 16
                        drain_q()
                        quant(w, nb, h_strip[0][:, 0:w],
                              hqbf[0][:, off:off + w],
                              pre_wait_=(lambda: rec.wait(
                                  "dve", sPE, hq_tr_pe)))
                        gq = quant_gen(
                            w, nb, h_strip[1][:, 0:w],
                            hqbf[1][:, off:off + w],
                            pre_wait=(lambda: rec.wait(
                                "dve", sPE, hq_tr_pe)))
                        for _ in range(4):
                            next(gq)
                        pend_q.append(gq)
                    drain_q(10)
                drain_q()
                prev_jp_gu_pe = rec.val(sPE)
                hq_all = rec.val(sDV)
                hq_tr = [None, None]
                for jj in range(2):
                    rec.wait("pe", sDV, hq_all)
                    tp_free[id(PS[2])] = (rec.val(sDV) if jp + T == 0
                                          else tp_free.get(id(PS[2]), (0, 0))[0],
                                          rec.val(sAC))
                    _, dv, ac = transpose_chunks(
                        [hqbf[jj][:, kk * 128:(kk + 1) * 128]
                         for kk in range(KI)],
                        lambda i0, cl, jj=jj: hq_fm[
                            :, (jj * KI + i0) * 128:(jj * KI + i0 + cl) * 128],
                        pre_evac_wait=(sPE, down_pe_last), alt=PS[2])
                    hq_tr[jj] = (dv, ac)
                hq_tr_pe = rec.val(sPE)
                tf2 = tp_free.get(id(PS[2]), (0, 0))
                bank_free[2] = (max(bank_free[2][0], tf2[1]),
                                max(bank_free[2][1], tf2[0]))
                wd_ld = {}

                def emit_wd(s42, guard, T=T, jp=jp):
                    if s42 >= S4:
                        return
                    if T == 0 and jp == 0:
                        rec.wait("pool", sSt, pp_store_all)
                    rec.wait("pool", sPE, guard)
                    rec.op("pool", lambda e, s4=s42, par=s42 % 2: e.dma_start(
                        out=wd_buf[par][:].rearrange("p (k c) -> p k c", c=256),
                        in_=wdT_d.rearrange("(k p) d -> p k d", p=128)
                        [:, :, s4 * 256:(s4 + 1) * 256]), inc=sWd, n=16)
                    wd_ld[s42] = rec.val(sWd)

                emit_wd(0, down_pe_last if jp == 1 else prev_pe_end)
                emit_wd(1, down_pe_last if jp == 1 else prev_pe_end)
                for s4 in range(S4):
                    par = s4 % 2
                    for jj in range(2):
                        j = jp * 2 + jj
                        bank = (2, 5, 1, 4)[(s4 * 2 + jj) % 4]
                        rec.wait("pe", sWd, wd_ld[s4])
                        rec.wait("pe", sDV, hq_tr[jj][0])
                        rec.wait("pe", sAC, hq_tr[jj][1])
                        rec.wait("pe", sAC, bank_free[bank][0])
                        rec.wait("pe", sDV, bank_free[bank][1])
                        for k in range(KI):
                            rec.op("pe", lambda e, s4=s4, jj=jj, k=k, bank=bank,
                                   par=par: e.matmul(
                                PS[bank][:, 0:256],
                                hq_fm[:, (jj * KI + k) * 128:(jj * KI + k + 1) * 128],
                                wd_buf[par][:, k * 256:(k + 1) * 256],
                                start=(k == 0), stop=(k == KI - 1)),
                                inc=sPE if k == KI - 1 else None)
                        pe_d = rec.val(sPE)
                        rec.wait("act", sPE, pe_d)
                        rec.op("act", lambda e, jj=jj, s4=s4, bank=bank: e.activation(
                            g_j[:, jj * D + s4 * 256: jj * D + (s4 + 1) * 256],
                            PS[bank][:, 0:256], AF.Silu), inc=sAC)
                        td_done = rec.val(sAC)
                        if jj == 1:
                            emit_wd(s4 + 2, pe_d)
                        bank_free[bank] = (rec.val(sAC), rec.val(sDV))
                down_pe_last = rec.val(sPE)
                g_done_ac = rec.val(sAC)

                for jj in range(2):
                    j = jp * 2 + jj
                    rec.wait("dve", sXr, xrd_ld[jj])
                    rec.wait("dve", sAC, g_done_ac)
                    rec.wait("dve", sDV, s_done)
                    rec.wait("dve", sOut, outb_used)
                    dve_edge(lambda e, j=j, jj=jj: e.tensor_tensor(
                        outb[:], s_tm[:, j * D:(j + 1) * D],
                        g_j[:, jj * D:(jj + 1) * D], ALU.mult))
                    rec.op("dve", lambda e, jj=jj: e.tensor_tensor(
                        outb[:], outb[:], xrd[jj][:], ALU.add), inc=sDV)
                    xrd_used[jj] = rec.val(sDV)
                    comb_done = rec.val(sDV)
                    rec.wait("sp", sDV, comb_done)
                    rec.op("sp", lambda e, j=j, t0=tok0: e.dma_start(
                        out=out_d[t0 + j * 128:t0 + (j + 1) * 128, :],
                        in_=outb[:]), inc=sOut, n=16)
                    outb_used = rec.val(sOut)
                if jp == 0:
                    for jj in range(2):
                        rec.wait("sp", sDV, xrd_used[jj])
                        rec.op("sp", lambda e, jj=jj, t0=tok0: e.dma_start(
                            out=xrd[jj][:],
                            in_=xin_d[t0 + (2 + jj) * 128:
                                      t0 + (3 + jj) * 128, :]),
                            inc=sXr, n=16)
                        xrd_ld[jj] = rec.val(sXr)

            if debug and T == 0:
                for nm, src in [("dbg_xnfm", xn_fm[:]), ("dbg_xqfm", xq_fm[:]),
                                ("dbg_normhs", normhs[:]), ("dbg_s", s_tm[:]),
                                ("dbg_g", g_j[:]), ("dbg_hqfm", hq_fm[:]),
                                ("dbg_a", a4[:]), ("dbg_rstd", tiny[:, 0:8]),
                                ("dbg_rstd2", tiny[:, 12:20]),
                                ("dbg_hst", hs_t[:]), ("dbg_ut", u_t[:]),
                                ("dbg_gat", gat_t[:]), ("dbg_v4", v4[:]),
                                ("dbg_carry", carry[:])]:
                    rec.op("sp", lambda e, nm=nm, src=src: e.dma_start(
                        out=dbg[nm], in_=src), inc=sOut, n=16)
                rec.op("sp", lambda e: e.dma_start(
                    out=dbg["dbg_waT"],
                    in_=waT_d.rearrange("d m q c -> d (m q c)")), inc=sOut, n=16)
                rec.op("sp", lambda e: e.dma_start(
                    out=dbg["dbg_wgT"], in_=wgT_d), inc=sOut, n=16)
                rec.op("sp", lambda e: e.dma_start(
                    out=dbg["dbg_wdT"], in_=wdT_d), inc=sOut, n=16)
                rec.wait("sp", sOut, rec.val(sOut))
                for eng in ("dve", "act", "pe", "pool"):
                    rec.wait(eng, sOut, rec.val(sOut))

            prev_pe_end = rec.val(sPE)
            prev_dve_end = rec.val(sDV)
            prev_act_end = rec.val(sAC)
            prev_po_end = carr_all - KD

        rec.wait("sp", sOut, rec.val(sOut))

        block.sync(rec.replay_body("sp"))
        block.tensor(rec.replay_body("pe"))
        block.vector(rec.replay_body("dve"))
        block.scalar(rec.replay_body("act"))
        block.gpsimd(rec.replay_body("pool"))

    return nc


def kernel(L0_attn_x, L0_attn_W_in, L0_attn_W_ig, L0_attn_b_ig, L0_attn_W_rg,
           L0_attn_b_rg, L0_attn_lam, L0_attn_W_out, L0_attn_h0, L0_attn_norm_w,
           L0_ffn_w_gate, L0_ffn_w_up, L0_ffn_w_down):
    x = np.ascontiguousarray(np.asarray(L0_attn_x, dtype=np.float32))
    ident = np.eye(128, dtype=ml_dtypes.bfloat16)
    ones = np.ones((128, 1), dtype=ml_dtypes.bfloat16)
    consts = np.tile(np.array(CONSTS, np.float32)[None, :], (128, 1))

    common = {
        "w_in": np.ascontiguousarray(np.asarray(L0_attn_W_in, np.float32)),
        "w_ig": np.ascontiguousarray(np.asarray(L0_attn_W_ig, np.float32)),
        "w_rg": np.ascontiguousarray(np.asarray(L0_attn_W_rg, np.float32)),
        "w_out": np.ascontiguousarray(np.asarray(L0_attn_W_out, np.float32)),
        "wg": np.ascontiguousarray(np.asarray(L0_ffn_w_gate, np.float32)),
        "wu": np.ascontiguousarray(np.asarray(L0_ffn_w_up, np.float32)),
        "wd": np.ascontiguousarray(np.asarray(L0_ffn_w_down, np.float32)),
        "b_ig": np.ascontiguousarray(np.asarray(L0_attn_b_ig, np.float32)),
        "b_rg": np.ascontiguousarray(np.asarray(L0_attn_b_rg, np.float32)),
        "lam": np.ascontiguousarray(np.asarray(L0_attn_lam, np.float32)),
        "norm_w": np.ascontiguousarray(np.asarray(L0_attn_norm_w, np.float32)),
        "h0": np.ascontiguousarray(np.asarray(L0_attn_h0, np.float32).reshape(D)),
        "ident": ident, "onescol": ones, "consts": consts,
    }
    in_maps = []
    for c in range(NCORE):
        m = dict(common)
        m["x"] = np.ascontiguousarray(x[c])
        in_maps.append(m)

    nc = build_nc(debug=DEBUG)
    res = run_bass_kernel_spmd(nc, in_maps, core_ids=list(range(NCORE)))
    out = np.stack([res.results[c]["out"] for c in range(NCORE)], axis=0)
    return out


# version 21
# speedup vs baseline: 1.1658x; 1.1658x over previous
import numpy as np
import ml_dtypes
from contextlib import ExitStack

import concourse.bass as bass
import concourse.mybir as mybir
from concourse.bass_utils import run_bass_kernel_spmd

F32 = mybir.dt.float32
BF16 = mybir.dt.bfloat16
AF = mybir.ActivationFunctionType
ALU = mybir.AluOpType

D = 1024
I = 2816
S = 4096
TT = 512
NTILE = S // TT
JT = TT // 128
KD = D // 128
KI = I // 128
NCORE = 8
NS = 11
S4 = 4

LN05 = float(np.log(0.5))
C_HALF = 0.5
C_ONE = 1.0
C_EPS_RMS = 1e-6
C_EPS_LN = 1e-5
C_INV1024 = 1.0 / 1024.0
C_NEGHALF = -0.5
C_NEG1 = -1.0
C_EPS_LOGA = 1e-8
CONSTS = [LN05, C_HALF, C_ONE, C_EPS_RMS, C_EPS_LN, C_INV1024, C_NEGHALF,
          C_NEG1, C_EPS_LOGA, -2.5, -3.5, -5.0]

DEBUG = False


class Rec:
    ENGS = ("sp", "pe", "dve", "act", "pool")

    def __init__(self):
        self.q = {e: [] for e in self.ENGS}
        self.cnt = {}

    def new_sem(self, handle):
        self.cnt[handle.name] = 0
        return handle

    def wait(self, eng, sem, val=None):
        if val is None:
            val = self.cnt[sem.name]
        if val > 0:
            self.q[eng].append(("w", sem, val))
        return val

    def op(self, eng, fn, inc=None, n=1):
        self.q[eng].append(("o", fn, inc, n))
        if inc is not None:
            self.cnt[inc.name] += n

    def val(self, sem):
        return self.cnt[sem.name]

    def replay_body(self, eng_name):
        items = self.q[eng_name]

        def body(eng):
            for it in items:
                if it[0] == "w":
                    eng.wait_ge(it[1], it[2])
                else:
                    _, fn, inc, n = it
                    ins = fn(eng)
                    if inc is not None:
                        ins.then_inc(inc, n)
        return body


def build_nc(debug=False, tap=False):
    nc = bass.Bass("TRN2", target_bir_lowering=False, debug=False,
                   detect_race_conditions=False)

    xin_d = nc.dram_tensor("x", [S, D], F32, kind="ExternalInput").ap()
    w_in_d = nc.dram_tensor("w_in", [D, D], F32, kind="ExternalInput").ap()
    w_ig_d = nc.dram_tensor("w_ig", [D, D], F32, kind="ExternalInput").ap()
    w_rg_d = nc.dram_tensor("w_rg", [D, D], F32, kind="ExternalInput").ap()
    w_out_d = nc.dram_tensor("w_out", [D, D], F32, kind="ExternalInput").ap()
    wg_d = nc.dram_tensor("wg", [I, D], F32, kind="ExternalInput").ap()
    wu_d = nc.dram_tensor("wu", [I, D], F32, kind="ExternalInput").ap()
    wd_d = nc.dram_tensor("wd", [D, I], F32, kind="ExternalInput").ap()
    big_d = nc.dram_tensor("b_ig", [D], F32, kind="ExternalInput").ap()
    brg_d = nc.dram_tensor("b_rg", [D], F32, kind="ExternalInput").ap()
    lam_d = nc.dram_tensor("lam", [D], F32, kind="ExternalInput").ap()
    nw_d = nc.dram_tensor("norm_w", [D], F32, kind="ExternalInput").ap()
    h0_d = nc.dram_tensor("h0", [D], F32, kind="ExternalInput").ap()
    ident_d = nc.dram_tensor("ident", [128, 128], BF16, kind="ExternalInput").ap()
    ones_d = nc.dram_tensor("onescol", [128, 1], BF16, kind="ExternalInput").ap()
    consts_d = nc.dram_tensor("consts", [128, len(CONSTS)], F32,
                              kind="ExternalInput").ap()
    out_d = nc.dram_tensor("out", [S, D], F32, kind="ExternalOutput").ap()

    waT_d = nc.dram_tensor("waT", [D, KD, 3, 128], BF16, kind="Internal").ap()
    woT_d = nc.dram_tensor("woT", [D, D], BF16, kind="Internal").ap()
    wgT_d = nc.dram_tensor("wgT", [D, I], BF16, kind="Internal").ap()
    wuT_d = nc.dram_tensor("wuT", [D, I], BF16, kind="Internal").ap()
    wdT_d = nc.dram_tensor("wdT", [I, D], BF16, kind="Internal").ap()

    taps = {}
    if tap:
        taps["tap_xnfm"] = nc.dram_tensor("tap_xnfm", [128, KD * TT], BF16,
                                          kind="ExternalOutput").ap()
        taps["tap_xqfm"] = nc.dram_tensor("tap_xqfm", [128, KD * TT], BF16,
                                          kind="ExternalOutput").ap()
        taps["tap_normhs"] = nc.dram_tensor("tap_normhs", [128, KD * TT], BF16,
                                            kind="ExternalOutput").ap()

    dbg = {}
    if debug:
        for name, shape, dt in [
            ("dbg_xnfm", [128, KD * TT], BF16),
            ("dbg_xqfm", [128, KD * TT], BF16),
            ("dbg_a", [128, 4 * TT], F32),
            ("dbg_normhs", [128, KD * TT], BF16),
            ("dbg_s", [128, JT * D], BF16),
            ("dbg_g", [128, 2 * D], BF16),
            ("dbg_hqfm", [128, 2 * KI * 128], BF16),
            ("dbg_rstd", [128, 8], F32),
            ("dbg_rstd2", [128, 8], F32),
            ("dbg_hst", [128, TT], F32),
            ("dbg_ut", [128, TT], F32),
            ("dbg_gat", [128, TT], F32),
            ("dbg_v4", [128, 4 * TT], BF16),
            ("dbg_carry", [128, KD], F32),
            ("dbg_waT", [D, KD * 3 * 128], BF16),
            ("dbg_wgT", [D, I], BF16),
            ("dbg_wdT", [I, D], BF16),
        ]:
            dbg[name] = nc.dram_tensor(name, shape, dt, kind="ExternalOutput").ap()

    rec = Rec()

    with ExitStack() as ctx:
        def sb(name, shape, dt):
            return ctx.enter_context(nc.sbuf_tensor(name, shape, dt))

        def psm(name, shape, dt):
            return ctx.enter_context(nc.psum_tensor(name, shape, dt))

        def sem(name):
            return rec.new_sem(ctx.enter_context(nc.semaphore(name)))

        xin = [sb(f"xin{i}", [128, D], F32) for i in range(4)]
        xn32 = sb("xn32", [128, D], F32)
        outb = xn32
        xnbf = sb("xnbf", [128, D], BF16)
        xqbf = sb("xqbf", [128, D], BF16)
        st12 = sb("st12", [128, JT * 12], F32)
        tiny = sb("tinyA", [128, 64], F32)
        QW = 1024
        q_zn = sb("q_zn", [128, QW], F32)
        q_A = sb("q_A", [128, QW], BF16)
        q_B = sb("q_B", [128, QW], BF16)
        q_C = sb("q_C", [128, QW], BF16)
        q_s5 = sb("q_s5", [128, QW], BF16)
        q_s6 = sb("q_s6", [128, QW], BF16)
        q_s7 = sb("q_s7", [128, QW], BF16)
        q_sgn = sb("q_sgn", [128, QW], BF16)
        q_sc = sb("q_sc", [128, 64], F32)
        q_inv = sb("q_inv", [128, 64], F32)
        q_hsc = sb("q_hsc", [128, 64], F32)
        xn_fm = sb("xn_fm", [128, KD * TT], BF16)
        xq_fm = sb("xq_fm", [128, KD * TT], BF16)
        a4 = sb("a4", [128, 4 * TT], F32)
        v4 = sb("v4", [128, 4 * TT], BF16)
        u_t = sb("u_t", [128, TT], F32)
        gat_t = sb("gat_t", [128, TT], F32)
        hs_t = sb("hs_t", [128, TT], F32)
        hs2_t = sb("hs2_t", [128, TT], BF16)
        normhs = sb("normhs", [128, KD * TT], BF16)
        carry = sb("carry", [128, KD], F32)
        h0c = sb("h0c", [128, KD], F32)
        wa_buf = [sb(f"wa{i}", [128, KD * 3 * 128], BF16) for i in range(2)]
        wo_buf = sb("wo_buf", [128, KD * D], BF16)
        wgu_buf = [sb(f"wgu{i}", [128, 2 * KD * 256], BF16) for i in range(2)]
        wd_buf = [sb(f"wd{i}", [128, KI * 256], BF16) for i in range(2)]
        tg_t = sb("tg_t", [128, 512], F32)
        sg_t = sb("sg_t", [128, 512], F32)
        ti_t = tg_t
        tr_t = sg_t
        h_strip = [sb(f"hstr{i}", [128, 1024], F32) for i in range(2)]
        hqbf = [sb(f"hqbf{i}", [128, I], BF16) for i in range(2)]
        hq_fm = sb("hq_fm", [128, 2 * KI * 128], BF16)
        g_j = sb("g_j", [128, 2 * D], BF16)
        s_tm = sb("s_tm", [128, JT * D], BF16)
        bih = sb("bih", [128, KD], F32)
        brh = sb("brh", [128, KD], F32)
        c4t = sb("c4t", [128, KD], F32)
        nwt = sb("nwt", [128, KD], F32)
        vtmp = sb("vtmp", [128, KD], F32)
        identt = sb("identt", [128, 128], BF16)
        onest = sb("onest", [128, 1], BF16)
        constt = sb("constt", [128, len(CONSTS)], F32)
        pp_row = [sb(f"pprow{i}", [128, D], F32) for i in range(2)]
        xrd = pp_row
        pp_q = hqbf[0]
        pp_ev = hqbf[1][:, 0:D]

        for j, cv in enumerate(CONSTS):
            nc.const_aps.register(F32, cv, constt[:, j:j + 1])

        PS = [psm(f"ps{i}", [128, 512], F32) for i in range(6)]
        PS_T = psm("ps_t", [128, 512], F32)
        PS_S = psm("ps_s", [128, 8], F32)

        sLd = sem("sLd")
        sWa = sem("sWa")
        sWo = sem("sWo")
        sWgu = sem("sWgu")
        sWd = sem("sWd")
        sX = sem("sX")
        sXr = sem("sXr")
        sSt = sem("sSt")
        sOut = sem("sOut")
        sPE = sem("sPE")
        sDV = sem("sDV")
        sAC = sem("sAC")
        sPO = sem("sPO")
        sTap = sem("sTap")

        block = ctx.enter_context(nc.Block())

        state = {"pst_free_dve": 0,
                 "pst_free_act": 0,
                 "pst_tog": 0,
                 "q_deq_done": 0,
                 "last_combine_act": 0}

        def dve_edge(fn):
            rec.op("dve", fn, inc=sDV)
            rec.wait("dve", sDV, rec.val(sDV))
            return rec.val(sDV)

        def dve_plain(fn):
            rec.op("dve", fn, inc=sDV)
            return rec.val(sDV)

        def act_edge(fn):
            rec.op("act", fn, inc=sAC)
            rec.wait("act", sAC, rec.val(sAC))
            return rec.val(sAC)

        def dma_relay(sem_, val):
            rec.wait("pool", sem_, val)
            rec.op("pool", lambda e: e.tensor_copy(
                hs2_t[:, 0:128], identt[:]))
            rec.op("pool", lambda e: e.tensor_copy(
                hs2_t[:, 128:256], identt[:]), inc=sPO)
            return rec.val(sPO)

        def quant_gen(width, nblk, src_f32, dst_bf, src_ready=None,
                      pre_wait=None):
            if pre_wait is not None:
                pre_wait()
            rec.wait("act", sDV, rec.val(sDV))
            if src_ready is not None:
                rec.wait("act", src_ready[0], src_ready[1])
            rec.op("act", lambda e, s=src_f32, w=width: e.sign(
                q_sgn[:, :w], s), inc=sAC)
            yield
            dve_edge(lambda e, s=src_f32, w=width, nb=nblk: e.tensor_reduce(
                q_sc[:, :nb].unsqueeze(2),
                s.rearrange("p (b s) -> p b s", s=16),
                mybir.AxisListType.X, ALU.max, apply_absolute_value=True))
            yield
            dve_edge(lambda e, nb=nblk: e.tensor_scalar(
                q_sc[:, :nb], q_sc[:, :nb], 1.0 / 6.0, 1e-12, ALU.mult, ALU.max))
            dve_edge(lambda e, nb=nblk: e.reciprocal(
                q_inv[:, :nb], q_sc[:, :nb]))
            dve_plain(lambda e, nb=nblk: e.tensor_scalar(
                q_hsc[:, :nb], q_sc[:, :nb], 0.5, None, ALU.mult))
            yield
            rec.op("dve", lambda e, s=src_f32, w=width, nb=nblk: e.tensor_tensor(
                q_zn[:, :w].rearrange("p (b s) -> p b s", s=16),
                s.rearrange("p (b s) -> p b s", s=16),
                q_inv[:, :nb].unsqueeze(2).broadcast_to([128, nb, 16]),
                ALU.mult), inc=sDV)
            zn_done = rec.val(sDV)
            rec.wait("act", sDV, zn_done)
            rec.op("act", lambda e, w=width: e.activation(
                q_zn[:, :w], q_zn[:, :w], AF.Abs), inc=sAC)
            abs_done = rec.val(sAC)
            rec.wait("dve", sAC, abs_done)
            yield
            rec.wait("act", sAC, abs_done)
            rec.op("act", lambda e, w=width: e.sign(q_s5[:, :w], q_zn[:, :w],
                                                    bias=-2.5))
            yield
            rec.op("act", lambda e, w=width: e.sign(q_s6[:, :w], q_zn[:, :w],
                                                    bias=-3.5))
            yield
            rec.op("act", lambda e, w=width: e.sign(q_s7[:, :w], q_zn[:, :w],
                                                    bias=-5.0), inc=sAC)
            s_done = rec.val(sAC)
            yield
            rec.wait("dve", sAC, abs_done)
            dve_plain(lambda e, w=width: e.tensor_scalar(
                q_A[:, :w], q_zn[:, :w], 0.25, 4.0, ALU.is_gt, ALU.add))
            yield
            dve_plain(lambda e, w=width: e.tensor_scalar(
                q_B[:, :w], q_zn[:, :w], 0.75, None, ALU.is_gt))
            yield
            dve_plain(lambda e, w=width: e.tensor_tensor(
                q_C[:, :w], q_A[:, :w], q_B[:, :w], ALU.add))
            yield
            dve_plain(lambda e, w=width: e.tensor_scalar(
                q_A[:, :w], q_zn[:, :w], 1.25, None, ALU.is_gt))
            yield
            dve_plain(lambda e, w=width: e.tensor_scalar(
                q_B[:, :w], q_zn[:, :w], 1.75, None, ALU.is_gt))
            yield
            dve_edge(lambda e, w=width: e.tensor_tensor(
                q_A[:, :w], q_A[:, :w], q_B[:, :w], ALU.add))
            yield
            dve_edge(lambda e, w=width: e.tensor_tensor(
                q_C[:, :w], q_C[:, :w], q_A[:, :w], ALU.add))
            yield
            rec.wait("dve", sAC, s_done)
            dve_plain(lambda e, w=width: e.tensor_tensor(
                q_A[:, :w], q_s5[:, :w], q_s6[:, :w], ALU.add))
            yield
            dve_edge(lambda e, w=width: e.tensor_tensor(
                q_C[:, :w], q_C[:, :w], q_A[:, :w], ALU.add))
            yield
            dve_plain(lambda e, w=width: e.scalar_tensor_tensor(
                q_A[:, :w], q_s7[:, :w], 2.0, q_C[:, :w], ALU.mult, ALU.add))
            yield
            dve_plain(lambda e, w=width: e.tensor_tensor(
                q_B[:, :w], q_A[:, :w], q_sgn[:, :w], ALU.mult))
            yield
            rec.op("dve", lambda e, d=dst_bf, w=width, nb=nblk: e.tensor_tensor(
                d.rearrange("p (b s) -> p b s", s=16),
                q_B[:, :w].rearrange("p (b s) -> p b s", s=16),
                q_hsc[:, :nb].unsqueeze(2).broadcast_to([128, nb, 16]),
                ALU.mult), inc=sDV)
            state["q_deq_done"] = rec.val(sDV)

        def quant(width, nblk, src_f32, dst_bf, src_ready=None,
                  pre_wait_=None):
            for _ in quant_gen(width, nblk, src_f32, dst_bf, src_ready,
                               pre_wait=pre_wait_):
                pass
            return rec.val(sDV)

        tp_free = {}

        def transpose_chunks(srcs, mk_out, pre_evac_wait=None, alt=None):
            banks = [PS_T] if alt is None else [PS_T, alt]
            i = 0
            p = 0
            while i < len(srcs):
                cl = min(4, len(srcs) - i)
                bank = banks[p % len(banks)]
                fd, fa = tp_free.get(id(bank), (0, 0))
                rec.wait("pe", sDV, fd)
                rec.wait("pe", sAC, fa)
                for c in range(cl):
                    rec.op("pe", lambda e, s=srcs[i + c], o=c * 128, b=bank:
                           e.matmul(b[:, o:o + 128], s, identt[:], start=True,
                                    stop=True), inc=sPE if c == cl - 1 else None)
                pe_c = rec.val(sPE)
                out_ap = mk_out(i, cl)
                src_ap = bank[:, :cl * 128]
                if out_ap.ndim == 3:
                    src_ap = src_ap.rearrange("p (k c) -> p k c", c=128)
                eng = "dve" if state["pst_tog"] == 0 else "act"
                state["pst_tog"] ^= 1
                rec.wait(eng, sPE, pe_c)
                if pre_evac_wait is not None:
                    rec.wait(eng, pre_evac_wait[0], pre_evac_wait[1])
                if eng == "dve":
                    rec.op("dve", lambda e, oa=out_ap, sa=src_ap: e.tensor_copy(
                        oa, sa), inc=sDV)
                    tp_free[id(bank)] = (rec.val(sDV), tp_free.get(id(bank), (0, 0))[1])
                else:
                    rec.op("act", lambda e, oa=out_ap, sa=src_ap: e.activation(
                        oa, sa, AF.Copy), inc=sAC)
                    tp_free[id(bank)] = (tp_free.get(id(bank), (0, 0))[0], rec.val(sAC))
                i += cl
                p += 1
            return rec.val(sPE), rec.val(sDV), rec.val(sAC)

        def ln_stats(src, jcol):
            rec.op("dve", lambda e, s=src, j=jcol: e.bn_stats(
                st12[:, j * 12:j * 12 + 6], s[:, 0:512]), inc=sDV)
            dve_edge(lambda e, s=src, j=jcol: e.bn_stats(
                st12[:, j * 12 + 6:j * 12 + 12], s[:, 512:1024]))
            return rec.val(sDV)

        def ln_combine():
            rec.wait("dve", sAC, state["last_combine_act"])
            v12 = st12[:].rearrange("p (j s) -> p j s", s=12)
            dve_edge(lambda e: e.tensor_reduce(
                tiny[:, 0:4].unsqueeze(2), v12[:, :, 1:12:3],
                mybir.AxisListType.X, ALU.add))
            dve_edge(lambda e: e.tensor_scalar(
                tiny[:, 0:4], tiny[:, 0:4], 0.25, None, ALU.mult))
            dve_edge(lambda e: e.tensor_reduce(
                tiny[:, 24:28].unsqueeze(2), v12[:, :, 2:12:3],
                mybir.AxisListType.X, ALU.add))
            dve_edge(lambda e: e.tensor_tensor(
                tiny[:, 44:60].rearrange("p (j s) -> p j s", s=4),
                v12[:, :, 1:12:3], v12[:, :, 1:12:3], ALU.mult))
            dve_edge(lambda e: e.tensor_reduce(
                tiny[:, 32:36].unsqueeze(2),
                tiny[:, 44:60].rearrange("p (j s) -> p j s", s=4),
                mybir.AxisListType.X, ALU.add))
            dve_edge(lambda e: e.tensor_scalar(
                tiny[:, 24:28], tiny[:, 24:28], C_INV1024, None, ALU.mult))
            dve_edge(lambda e: e.scalar_tensor_tensor(
                tiny[:, 32:36], tiny[:, 32:36], 0.25, tiny[:, 24:28],
                ALU.mult, ALU.add))
            dve_edge(lambda e: e.tensor_tensor(
                tiny[:, 28:32], tiny[:, 0:4], tiny[:, 0:4], ALU.mult))
            dve_edge(lambda e: e.tensor_tensor(
                tiny[:, 24:28], tiny[:, 32:36], tiny[:, 28:32], ALU.subtract))
            var_done = rec.val(sDV)
            rec.wait("act", sDV, var_done)
            act_edge(lambda e: e.activation(
                tiny[:, 36:40], tiny[:, 24:28], AF.Ln, bias=C_EPS_LN))
            rec.op("act", lambda e: e.activation(
                tiny[:, 4:8], tiny[:, 36:40], AF.Exp, scale=C_NEGHALF),
                inc=sAC)
            state["last_combine_act"] = rec.val(sAC)
            return rec.val(sAC)

        rec.op("sp", lambda e: e.dma_start(out=constt[:], in_=consts_d), inc=sLd, n=16)
        rec.op("sp", lambda e: e.dma_start(out=identt[:], in_=ident_d), inc=sLd, n=16)
        rec.op("sp", lambda e: e.dma_start(out=onest[:], in_=ones_d), inc=sLd, n=16)
        for dst, src in [(bih, big_d), (brh, brg_d), (vtmp, lam_d),
                         (nwt, nw_d), (h0c, h0_d)]:
            rec.op("sp", lambda e, d=dst, s=src: e.dma_start(
                out=d[:], in_=s.rearrange("(m p) -> p m", p=128),
                allow_slow_non_contiguous=True), inc=sLd, n=16)
        xload_done = [0] * JT
        for j in range(JT):
            rec.op("sp", lambda e, j=j: e.dma_start(
                out=xin[j][:], in_=xin_d[j * 128:(j + 1) * 128, :]),
                inc=sX, n=16)
            xload_done[j] = rec.val(sX)
        vec_ld = rec.val(sLd)
        vec_rdy = dma_relay(sLd, vec_ld)
        rec.wait("dve", sPO, vec_rdy)
        dve_edge(lambda e: e.tensor_scalar(bih[:], bih[:], 0.5, None, ALU.mult))
        dve_edge(lambda e: e.tensor_scalar(brh[:], brh[:], 0.5, None, ALU.mult))
        rec.op("dve", lambda e: e.tensor_copy(carry[:], h0c[:]), inc=sDV)
        rec.wait("act", sDV, rec.val(sDV))
        rec.op("act", lambda e: e.activation(c4t[:], vtmp[:], AF.Tanh,
                                             scale=C_HALF), inc=sAC)
        rec.wait("dve", sAC, rec.val(sAC))
        rec.op("dve", lambda e: e.tensor_scalar(c4t[:], c4t[:], 0.5, 0.5,
                                                ALU.mult, ALU.add), inc=sDV)
        rec.wait("act", sDV, rec.val(sDV))
        rec.op("act", lambda e: e.activation(c4t[:], c4t[:], AF.Ln,
                                             bias=C_EPS_LOGA), inc=sAC)
        rec.wait("dve", sAC, rec.val(sAC))
        rec.op("dve", lambda e: e.tensor_scalar(c4t[:], c4t[:], 4.0, None,
                                                ALU.mult), inc=sDV)

        def pp_attn_weight(w_dram, pslot, halve=False):
            rec.wait("dve", sPE, rec.val(sPE))
            for r in range(KD):
                q_ = "sp" if r % 2 == 0 else "act"
                rec.wait(q_, sDV, rec.val(sDV))
                rec.op(q_, lambda e, r=r, w=w_dram: e.dma_start(
                    out=pp_row[r % 2][:, :D], in_=w[r * 128:(r + 1) * 128, :]),
                    inc=sLd, n=16)
                rec.wait("dve", sPO, dma_relay(sLd, rec.val(sLd)))
                if halve:
                    rec.op("dve", lambda e, r=r: e.tensor_scalar(
                        wo_buf[:, r * D:(r + 1) * D], pp_row[r % 2][:, :D],
                        0.5, None, ALU.mult), inc=sDV)
                else:
                    rec.op("dve", lambda e, r=r: e.tensor_copy(
                        wo_buf[:, r * D:(r + 1) * D], pp_row[r % 2][:, :D]),
                        inc=sDV)
            cast_done = rec.val(sDV)
            for k in range(KD):
                rec.wait("pe", sDV, cast_done)
                st_now = rec.val(sSt)
                _, dv, ac = transpose_chunks(
                    [wo_buf[:, r * D + k * 128: r * D + (k + 1) * 128]
                     for r in range(KD)],
                    lambda i0, cl: pp_ev[:, i0 * 128:(i0 + cl) * 128],
                    pre_evac_wait=(sSt, st_now), alt=PS[5])
                rec.wait("sp", sDV, dv)
                rec.wait("sp", sAC, ac)
                if pslot >= 0:
                    outap = waT_d[k * 128:(k + 1) * 128, :, pslot, :]
                    inap = pp_ev.rearrange("p (r c) -> p r c", c=128)
                else:
                    outap = woT_d[k * 128:(k + 1) * 128, :]
                    inap = pp_ev
                rec.op("sp", lambda e, o=outap, i=inap: e.dma_start(
                    out=o, in_=i), inc=sSt, n=16)

        pp_attn_weight(w_in_d, 0, halve=True)
        pp_attn_weight(w_ig_d, 1)
        pp_attn_weight(w_rg_d, 2)
        pp_attn_weight(w_out_d, -1)
        pp_st_attn = rec.val(sSt)

        pp_src_done = [0, 0]

        def pp_ffn_gu(w_dram, scratch):
            for r in range(KI):
                q_ = "sp" if r % 2 == 0 else "act"
                rec.wait(q_, sDV, pp_src_done[r % 2])
                rec.op(q_, lambda e, r=r, w=w_dram: e.dma_start(
                    out=pp_row[r % 2][:, :D], in_=w[r * 128:(r + 1) * 128, :]),
                    inc=sLd, n=16)
                row_rdy = dma_relay(sLd, rec.val(sLd))
                rec.wait("dve", sPO, row_rdy)
                rec.wait("dve", sPE, rec.val(sPE))
                gq = quant_gen(D, 64, pp_row[r % 2][:, :D], pp_q[:, :D],
                               src_ready=(sPO, row_rdy))
                for _ in range(5):
                    next(gq)
                pp_src_done[r % 2] = rec.val(sDV)
                for _ in gq:
                    pass
                qd = rec.val(sDV)
                rec.wait("pe", sDV, qd)
                st_now = rec.val(sSt)
                _, dv, ac = transpose_chunks(
                    [pp_q[:, k * 128:(k + 1) * 128] for k in range(KD)],
                    lambda i0, cl: pp_ev[:, i0 * 128:(i0 + cl) * 128],
                    pre_evac_wait=(sSt, st_now), alt=PS[5])
                rec.wait("sp", sDV, dv)
                rec.wait("sp", sAC, ac)
                rec.op("sp", lambda e, r=r, sc=scratch: e.dma_start(
                    out=sc.rearrange("(k p) i -> p k i", p=128)[
                        :, :, r * 128:(r + 1) * 128],
                    in_=pp_ev.rearrange("p (k c) -> p k c", c=128)),
                    inc=sSt, n=16)

        pp_ffn_gu(wg_d, wgT_d)
        pp_ffn_gu(wu_d, wuT_d)
        pp_st_gu = rec.val(sSt)

        for r in range(KD):
            rec.wait("dve", sPE, rec.val(sPE))
            for t3, (off, w, nb) in enumerate(
                    ((0, 960, 60), (960, 960, 60), (1920, 896, 56))):
                q_ = "sp" if t3 % 2 == 0 else "act"
                rec.wait(q_, sDV, pp_src_done[t3 % 2])
                rec.op(q_, lambda e, r=r, off=off, w=w, t3=t3: e.dma_start(
                    out=pp_row[t3 % 2][:, :w],
                    in_=wd_d[r * 128:(r + 1) * 128, off:off + w]),
                    inc=sLd, n=16)
                row_rdy = dma_relay(sLd, rec.val(sLd))
                rec.wait("dve", sPO, row_rdy)
                gq = quant_gen(w, nb, pp_row[t3 % 2][:, :w],
                               pp_q[:, off:off + w], src_ready=(sPO, row_rdy))
                for _ in range(5):
                    next(gq)
                pp_src_done[t3 % 2] = rec.val(sDV)
                for _ in gq:
                    pass
            qd = rec.val(sDV)
            for kg, nk in ((0, 8), (1, 8), (2, 6)):
                rec.wait("pe", sDV, qd)
                st_now = rec.val(sSt)
                _, dv, ac = transpose_chunks(
                    [pp_q[:, (kg * 8 + kk) * 128:(kg * 8 + kk + 1) * 128]
                     for kk in range(nk)],
                    lambda i0, cl: pp_ev[:, i0 * 128:(i0 + cl) * 128],
                    pre_evac_wait=(sSt, st_now), alt=PS[5])
                rec.wait("sp", sDV, dv)
                rec.wait("sp", sAC, ac)
                rec.op("sp", lambda e, r=r, kg=kg, nk=nk: e.dma_start(
                    out=wdT_d.rearrange("(k p) d -> p k d", p=128)[
                        :, kg * 8:kg * 8 + nk, r * 128:(r + 1) * 128],
                    in_=pp_ev[:, :nk * 128].rearrange("p (k c) -> p k c", c=128)),
                    inc=sSt, n=16)

        pp_store_all = rec.val(sSt)
        po_base = rec.val(sPO)

        for j in range(JT):
            rec.wait("dve", sX, xload_done[j])
            ln_stats(xin[j], j)
        rstd_next_done = ln_combine()

        xrd_used = [rec.val(sDV), rec.val(sDV)]
        outb_used = 0
        woT_loaded = 0
        carr_done = 0
        prev_pe_end = 0
        prev_dve_end = 0
        prev_act_end = 0
        prev_po_end = 0

        for T in range(NTILE):
            tok0 = T * TT
            rec.wait("pe", sDV, prev_dve_end)
            rec.wait("pe", sAC, prev_act_end)
            rec.wait("dve", sPE, prev_pe_end)
            rec.wait("dve", sAC, prev_act_end)
            rec.wait("act", sPE, prev_pe_end)
            rec.wait("act", sDV, prev_dve_end)
            rec.wait("pool", sDV, prev_dve_end)
            rec.wait("dve", sOut, rec.val(sOut))
            if tap and T == 1:
                rec.wait("dve", sTap, rec.val(sTap))
                rec.wait("act", sTap, rec.val(sTap))

            xrd_ld = [0, 0]
            for jj in range(2):
                rec.wait("sp", sDV, xrd_used[jj])
                rec.op("sp", lambda e, jj=jj, t0=tok0: e.dma_start(
                    out=xrd[jj][:],
                    in_=xin_d[t0 + jj * 128:t0 + (jj + 1) * 128, :]),
                    inc=sXr, n=16)
                xrd_ld[jj] = rec.val(sXr)

            if T == 0:
                rec.wait("sp", sSt, pp_store_all)
                rec.op("sp", lambda e: e.dma_start(
                    out=wo_buf[:].rearrange("p (k h) -> p k h", h=D),
                    in_=woT_d.rearrange("(k p) h -> p k h", p=128)),
                    inc=sWo, n=16)
                woT_loaded = rec.val(sWo)
            rstd_this = rstd_next_done
            pe_xn_last = 0
            pe_xq_last = 0
            for j in range(JT):
                rec.wait("dve", sX, xload_done[j])
                rec.wait("dve", sAC, rstd_this)
                dve_plain(lambda e, j=j: e.tensor_scalar(
                    xn32[:], xin[j][:], tiny[:, j:j + 1],
                    tiny[:, 4 + j:5 + j], ALU.subtract, ALU.mult))
                if T + 1 < NTILE:
                    rec.wait("sp", sDV, rec.val(sDV))
                    rec.op("sp", lambda e, j=j, nt0=tok0 + TT: e.dma_start(
                        out=xin[j][:],
                        in_=xin_d[nt0 + j * 128:nt0 + (j + 1) * 128, :]),
                        inc=sX, n=16)
                    xload_done[j] = rec.val(sX)
                rec.wait("dve", sPE, pe_xn_last)
                rec.op("dve", lambda e: e.tensor_copy(xnbf[:], xn32[:]), inc=sDV)
                xnbf_done = rec.val(sDV)
                rec.wait("dve", sPE, pe_xq_last)
                quant(D, 64, xn32[:], xqbf[:])
                xq_done = rec.val(sDV)
                rec.wait("pe", sDV, xnbf_done)
                pe_xn_last, _, _ = transpose_chunks(
                    [xnbf[:, k * 128:(k + 1) * 128] for k in range(KD)],
                    lambda i0, cl, j=j: xn_fm[:].rearrange(
                        "p (k t) -> p k t", t=TT)
                    [:, i0:i0 + cl, j * 128:(j + 1) * 128], alt=PS[5])
                rec.wait("pe", sDV, xq_done)
                pe_xq_last, _, _ = transpose_chunks(
                    [xqbf[:, k * 128:(k + 1) * 128] for k in range(KD)],
                    lambda i0, cl, j=j: xq_fm[:].rearrange(
                        "p (k t) -> p k t", t=TT)
                    [:, i0:i0 + cl, j * 128:(j + 1) * 128], alt=PS[5])
            xnfm_all = (rec.val(sDV), rec.val(sAC))
            xqfm_all = (rec.val(sDV), rec.val(sAC))
            if tap and T == 0:
                rec.wait("sp", sDV, xnfm_all[0])
                rec.wait("sp", sAC, xnfm_all[1])
                rec.op("sp", lambda e: e.dma_start(
                    out=taps["tap_xnfm"], in_=xn_fm[:]), inc=sTap, n=16)
                rec.op("sp", lambda e: e.dma_start(
                    out=taps["tap_xqfm"], in_=xq_fm[:]), inc=sTap, n=16)

            wa_ld = [0] * KD

            def emit_wa(m2, guard, T=T):
                if m2 >= KD:
                    return
                if T == 0:
                    rec.wait("pool", sSt, pp_st_attn)
                rec.wait("pool", sPE, guard)
                rec.op("pool", lambda e, m=m2, par=m2 % 2: e.dma_start(
                    out=wa_buf[par][:],
                    in_=waT_d.rearrange("(k p) m q c -> p k m q c", p=128)
                    [:, :, m, :, :]), inc=sWa, n=16)
                wa_ld[m2] = rec.val(sWa)

            emit_wa(0, prev_pe_end)
            emit_wa(1, prev_pe_end)
            proj_done_m = []
            scanned = [0] * KD
            hs2_done_m = [0] * KD
            mss_pe = [0] * KD
            for m in range(KD):
                bank = (m % 2) * 3
                par = m % 2
                mg = m % 4
                rec.wait("pe", sWa, wa_ld[m])
                rec.wait("pe", sDV, xnfm_all[0])
                rec.wait("pe", sAC, xnfm_all[1])
                if m >= 2:
                    rec.wait("pe", sAC, proj_done_m[m - 2][0])
                    rec.wait("pe", sDV, proj_done_m[m - 2][1])
                for q in range(3):
                    for k in range(KD):
                        rec.op("pe", lambda e, q=q, k=k, bank=bank, par=par: e.matmul(
                            PS[bank + q][:],
                            wa_buf[par][:].rearrange(
                                "p (k q c) -> p k q c", k=KD, q=3)[:, k, q, :],
                            xn_fm[:, k * TT:(k + 1) * TT],
                            start=(k == 0), stop=(k == KD - 1)),
                            inc=sPE if (q == 2 and k == KD - 1) else None)
                pe_m = rec.val(sPE)
                rec.wait("act", sPE, pe_m)
                if m >= 1:
                    rec.wait("act", sDV, proj_done_m[m - 1][1])
                if m >= 4:
                    rec.wait("act", sDV, scanned[m - 4])
                rec.op("act", lambda e, m=m, bank=bank: e.activation(
                    ti_t[:], PS[bank + 1][:], AF.Tanh, bias=bih[:, m:m + 1],
                    scale=C_HALF))
                act_edge(lambda e, m=m, bank=bank: e.activation(
                    tr_t[:], PS[bank + 2][:], AF.Tanh, bias=brh[:, m:m + 1],
                    scale=C_HALF))
                rec.op("act", lambda e, m=m, mg=mg: e.activation(
                    a4[:, mg * TT:(mg + 1) * TT], tr_t[:], AF.Exp,
                    bias=c4t[:, m:m + 1], scale=c4t[:, m:m + 1]), inc=sAC)
                act_m = rec.val(sAC)
                emit_wa(m + 2, pe_m)
                rec.wait("dve", sAC, act_m)
                if m >= 4:
                    rec.wait("dve", sAC, hs2_done_m[m - 4])
                dve_plain(lambda e, bank=bank, mg=mg: e.scalar_tensor_tensor(
                    v4[:, mg * TT:(mg + 1) * TT], ti_t[:], 1.0, PS[bank + 0][:],
                    ALU.add, ALU.mult))
                dve_m = rec.val(sDV)
                proj_done_m.append((act_m, dve_m))

                if mg == 3:
                    for mm in range(m - 3, m + 1):
                        mgg = mm % 4
                        if mm >= 1:
                            rec.wait("act", sDV, scanned[mm - 1])
                            rec.wait("act", sPE, mss_pe[mm - 1])
                        act_edge(lambda e, mgg=mgg: e.activation(
                            u_t[:], a4[:, mgg * TT:(mgg + 1) * TT], AF.Square))
                        act_edge(lambda e: e.activation(
                            u_t[:], u_t[:], AF.Copy, bias=C_ONE, scale=C_NEG1))
                        act_edge(lambda e: e.activation(
                            u_t[:], u_t[:], AF.Ln, bias=C_EPS_RMS))
                        rec.op("act", lambda e: e.activation(
                            u_t[:], u_t[:], AF.Exp, scale=C_HALF),
                            inc=sAC)
                        u_done = rec.val(sAC)
                        rec.wait("dve", sAC, u_done)
                        dve_edge(lambda e, mgg=mgg: e.tensor_tensor(
                            gat_t[:], u_t[:], v4[:, mgg * TT:(mgg + 1) * TT],
                            ALU.mult))
                        if mm >= 1:
                            rec.wait("dve", sAC, hs2_done_m[mm - 1])
                            rec.wait("dve", sPO, po_base + T * KD + mm)
                        elif T > 0:
                            rec.wait("dve", sPO, po_base + T * KD)
                        rec.op("dve", lambda e, mm=mm: e.tensor_tensor_scan(
                            hs_t[:], a4[:, (mm % 4) * TT:((mm % 4) + 1) * TT],
                            gat_t[:], carry[:, mm:mm + 1], ALU.mult, ALU.add),
                            inc=sDV)
                        scanned[mm] = rec.val(sDV)
                        rec.wait("pool", sDV, scanned[mm])
                        rec.op("pool", lambda e, mm=mm: e.tensor_copy(
                            carry[:, mm:mm + 1], hs_t[:, TT - 1:TT]), inc=sPO)
                        rec.wait("act", sDV, scanned[mm])
                        rec.op("act", lambda e: e.activation(
                            hs2_t[:], hs_t[:], AF.Square), inc=sAC)
                        hs2_done_m[mm] = rec.val(sAC)
                        rec.wait("dve", sDV, scanned[mm])
                        rec.op("dve", lambda e, mm=mm: e.tensor_scalar(
                            normhs[:, mm * TT:(mm + 1) * TT], hs_t[:],
                            nwt[:, mm:mm + 1], None, ALU.mult), inc=sDV)
                        rec.wait("pe", sAC, hs2_done_m[mm])
                        for j2 in range(JT):
                            rec.op("pe", lambda e, j2=j2, mm=mm: e.matmul(
                                PS_S[:, j2:j2 + 1],
                                hs2_t[:, j2 * 128:(j2 + 1) * 128], onest[:],
                                start=(mm == 0 and j2 == 0),
                                stop=(mm == KD - 1),
                                skip_group_check=True),
                                inc=sPE if j2 == JT - 1 else None)
                        mss_pe[mm] = rec.val(sPE)
            carr_all = rec.val(sPO)
            normhs_done = rec.val(sDV)
            mss_done = rec.val(sPE)
            if tap and T == 0:
                rec.wait("sp", sDV, normhs_done)
                rec.op("sp", lambda e: e.dma_start(
                    out=taps["tap_normhs"], in_=normhs[:]), inc=sTap, n=16)

            rec.wait("dve", sPE, mss_done)
            dve_edge(lambda e: e.tensor_copy(tiny[:, 12:16], PS_S[:, 0:4]))
            rec.wait("act", sDV, rec.val(sDV))
            act_edge(lambda e: e.activation(
                tiny[:, 40:44], tiny[:, 12:16], AF.Ln, bias=C_EPS_RMS,
                scale=C_INV1024))
            rec.op("act", lambda e: e.activation(
                tiny[:, 16:20], tiny[:, 40:44], AF.Exp,
                scale=C_NEGHALF), inc=sAC)
            rstd_rms_done = rec.val(sAC)

            s_parts = []
            s_bank_free = {}
            for j in range(JT):
                for dch in range(2):
                    bank = 0 if (j * 2 + dch) % 2 == 0 else 3
                    rec.wait("pe", sWo, woT_loaded)
                    rec.wait("pe", sDV, normhs_done)
                    if len(s_parts) >= 2:
                        rec.wait("pe", sDV, s_parts[-2])
                    for k in range(KD):
                        rec.op("pe", lambda e, j=j, dch=dch, k=k, bank=bank: e.matmul(
                            PS[bank][:],
                            normhs[:, k * TT + j * 128: k * TT + (j + 1) * 128],
                            wo_buf[:, k * D + dch * 512: k * D + (dch + 1) * 512],
                            start=(k == 0), stop=(k == KD - 1)),
                            inc=sPE if k == KD - 1 else None)
                    pe_s = rec.val(sPE)
                    rec.wait("dve", sPE, pe_s)
                    rec.wait("dve", sAC, rstd_rms_done)
                    rec.op("dve", lambda e, j=j, dch=dch, bank=bank: e.tensor_scalar(
                        s_tm[:, j * D + dch * 512: j * D + (dch + 1) * 512],
                        PS[bank][:], tiny[:, 16 + j:17 + j], None, ALU.mult),
                        inc=sDV)
                    s_parts.append(rec.val(sDV))
                    s_bank_free[bank] = (0, rec.val(sDV))
            s_done = s_parts[-1]

            if T + 1 < NTILE:
                for j in range(JT):
                    rec.wait("dve", sX, xload_done[j])
                    ln_stats(xin[j], j)
                rstd_next_done = ln_combine()

            g_done = 0
            prev_jp_gu_pe = 0
            pend_q = []

            def drain_q(k=None):
                n = 0
                while pend_q and (k is None or n < k):
                    try:
                        next(pend_q[0])
                    except StopIteration:
                        pend_q.pop(0)
                    n += 1

            bank_free = {b: (proj_done_m[-1][0],
                             max(s_done, s_bank_free.get(b, (0, 0))[1]))
                         for b in range(6)}
            hq_tr_pe = 0
            down_pe_last = 0
            for jp in range(2):
                wgu_ld = {}

                def emit_wgu(sl2, guard, T=T, jp=jp):
                    if sl2 >= NS:
                        return
                    if T == 0 and jp == 0:
                        rec.wait("pool", sSt, pp_st_gu)
                    rec.wait("pool", sPE, guard)
                    rec.op("pool", lambda e, sl=sl2, par=sl2 % 2: e.dma_start(
                        out=wgu_buf[par][:].rearrange(
                            "p (k c) -> p k c", c=512)[:, :, 0:256],
                        in_=wgT_d.rearrange("(k p) i -> p k i", p=128)
                        [:, :, sl * 256:(sl + 1) * 256]), inc=sWgu, n=16)
                    rec.op("pool", lambda e, sl=sl2, par=sl2 % 2: e.dma_start(
                        out=wgu_buf[par][:].rearrange(
                            "p (k c) -> p k c", c=512)[:, :, 256:512],
                        in_=wuT_d.rearrange("(k p) i -> p k i", p=128)
                        [:, :, sl * 256:(sl + 1) * 256]), inc=sWgu, n=16)
                    wgu_ld[sl2] = rec.val(sWgu)

                gu_guard0 = prev_jp_gu_pe if jp == 1 else prev_pe_end
                emit_wgu(0, gu_guard0)
                emit_wgu(1, gu_guard0)
                for sl in range(NS):
                    par = sl % 2
                    for jj in range(2):
                        j = jp * 2 + jj
                        bank = jj * 3 + sl % 2
                        rec.wait("pe", sWgu, wgu_ld[sl])
                        rec.wait("pe", sDV, xqfm_all[0])
                        rec.wait("pe", sAC, xqfm_all[1])
                        rec.wait("pe", sAC, bank_free[bank][0])
                        rec.wait("pe", sDV, bank_free[bank][1])
                        for k in range(KD):
                            rec.op("pe", lambda e, j=j, sl=sl, k=k,
                                   bank=bank, par=par: e.matmul(
                                PS[bank][:],
                                xq_fm[:, k * TT + j * 128: k * TT + (j + 1) * 128],
                                wgu_buf[par][:].rearrange(
                                    "p (k c) -> p k c", c=512)[:, k, :],
                                start=(k == 0), stop=(k == KD - 1)),
                                inc=sPE if k == KD - 1 else None)
                        pe_gu = rec.val(sPE)
                        rec.wait("act", sPE, pe_gu)
                        rec.wait("act", sDV, rec.val(sDV))
                        rec.op("act", lambda e, bank=bank: e.activation(
                            sg_t[:, 0:256], PS[bank][:, 0:256], AF.Silu),
                            inc=sAC)
                        tg_done = rec.val(sAC)
                        if jj == 1:
                            emit_wgu(sl + 2, pe_gu)
                        rec.wait("dve", sAC, tg_done)
                        dve_plain(lambda e, jj=jj, sl=sl, bank=bank: e.tensor_tensor(
                            h_strip[jj][:, (sl % 4) * 256:(sl % 4 + 1) * 256],
                            sg_t[:, 0:256], PS[bank][:, 256:512], ALU.mult))
                        bank_free[bank] = (rec.val(sAC), rec.val(sDV))
                    if sl in (3, 7, 10):
                        strip = sl // 4
                        off = strip * 1024
                        w = 1024 if sl < 10 else 768
                        nb = w // 16
                        drain_q()
                        quant(w, nb, h_strip[0][:, 0:w],
                              hqbf[0][:, off:off + w],
                              pre_wait_=(lambda: rec.wait(
                                  "dve", sPE, hq_tr_pe)))
                        gq = quant_gen(
                            w, nb, h_strip[1][:, 0:w],
                            hqbf[1][:, off:off + w],
                            pre_wait=(lambda: rec.wait(
                                "dve", sPE, hq_tr_pe)))
                        for _ in range(4):
                            next(gq)
                        pend_q.append(gq)
                    drain_q(10)
                drain_q()
                prev_jp_gu_pe = rec.val(sPE)
                hq_all = rec.val(sDV)
                hq_tr = [None, None]
                for jj in range(2):
                    rec.wait("pe", sDV, hq_all)
                    tp_free[id(PS[2])] = (rec.val(sDV) if jp + T == 0
                                          else tp_free.get(id(PS[2]), (0, 0))[0],
                                          rec.val(sAC))
                    _, dv, ac = transpose_chunks(
                        [hqbf[jj][:, kk * 128:(kk + 1) * 128]
                         for kk in range(KI)],
                        lambda i0, cl, jj=jj: hq_fm[
                            :, (jj * KI + i0) * 128:(jj * KI + i0 + cl) * 128],
                        pre_evac_wait=(sPE, down_pe_last), alt=PS[2])
                    hq_tr[jj] = (dv, ac)
                hq_tr_pe = rec.val(sPE)
                tf2 = tp_free.get(id(PS[2]), (0, 0))
                bank_free[2] = (max(bank_free[2][0], tf2[1]),
                                max(bank_free[2][1], tf2[0]))
                wd_ld = {}

                def emit_wd(s42, guard, T=T, jp=jp):
                    if s42 >= S4:
                        return
                    if T == 0 and jp == 0:
                        rec.wait("pool", sSt, pp_store_all)
                    rec.wait("pool", sPE, guard)
                    rec.op("pool", lambda e, s4=s42, par=s42 % 2: e.dma_start(
                        out=wd_buf[par][:].rearrange("p (k c) -> p k c", c=256),
                        in_=wdT_d.rearrange("(k p) d -> p k d", p=128)
                        [:, :, s4 * 256:(s4 + 1) * 256]), inc=sWd, n=16)
                    wd_ld[s42] = rec.val(sWd)

                emit_wd(0, down_pe_last if jp == 1 else prev_pe_end)
                emit_wd(1, down_pe_last if jp == 1 else prev_pe_end)
                for s4 in range(S4):
                    par = s4 % 2
                    for jj in range(2):
                        j = jp * 2 + jj
                        bank = (2, 5, 1, 4)[(s4 * 2 + jj) % 4]
                        rec.wait("pe", sWd, wd_ld[s4])
                        rec.wait("pe", sDV, hq_tr[jj][0])
                        rec.wait("pe", sAC, hq_tr[jj][1])
                        rec.wait("pe", sAC, bank_free[bank][0])
                        rec.wait("pe", sDV, bank_free[bank][1])
                        for k in range(KI):
                            rec.op("pe", lambda e, s4=s4, jj=jj, k=k, bank=bank,
                                   par=par: e.matmul(
                                PS[bank][:, 0:256],
                                hq_fm[:, (jj * KI + k) * 128:(jj * KI + k + 1) * 128],
                                wd_buf[par][:, k * 256:(k + 1) * 256],
                                start=(k == 0), stop=(k == KI - 1)),
                                inc=sPE if k == KI - 1 else None)
                        pe_d = rec.val(sPE)
                        rec.wait("act", sPE, pe_d)
                        rec.op("act", lambda e, jj=jj, s4=s4, bank=bank: e.activation(
                            g_j[:, jj * D + s4 * 256: jj * D + (s4 + 1) * 256],
                            PS[bank][:, 0:256], AF.Silu), inc=sAC)
                        td_done = rec.val(sAC)
                        if jj == 1:
                            emit_wd(s4 + 2, pe_d)
                        bank_free[bank] = (rec.val(sAC), rec.val(sDV))
                down_pe_last = rec.val(sPE)
                g_done_ac = rec.val(sAC)

                for jj in range(2):
                    j = jp * 2 + jj
                    rec.wait("dve", sXr, xrd_ld[jj])
                    rec.wait("dve", sAC, g_done_ac)
                    rec.wait("dve", sDV, s_done)
                    rec.wait("dve", sOut, outb_used)
                    dve_plain(lambda e, j=j, jj=jj: e.tensor_tensor(
                        outb[:], s_tm[:, j * D:(j + 1) * D],
                        g_j[:, jj * D:(jj + 1) * D], ALU.mult))
                    rec.op("dve", lambda e, jj=jj: e.tensor_tensor(
                        outb[:], outb[:], xrd[jj][:], ALU.add), inc=sDV)
                    xrd_used[jj] = rec.val(sDV)
                    comb_done = rec.val(sDV)
                    rec.wait("sp", sDV, comb_done)
                    rec.op("sp", lambda e, j=j, t0=tok0: e.dma_start(
                        out=out_d[t0 + j * 128:t0 + (j + 1) * 128, :],
                        in_=outb[:]), inc=sOut, n=16)
                    outb_used = rec.val(sOut)
                if jp == 0:
                    for jj in range(2):
                        rec.wait("sp", sDV, xrd_used[jj])
                        rec.op("sp", lambda e, jj=jj, t0=tok0: e.dma_start(
                            out=xrd[jj][:],
                            in_=xin_d[t0 + (2 + jj) * 128:
                                      t0 + (3 + jj) * 128, :]),
                            inc=sXr, n=16)
                        xrd_ld[jj] = rec.val(sXr)

            if debug and T == 0:
                for nm, src in [("dbg_xnfm", xn_fm[:]), ("dbg_xqfm", xq_fm[:]),
                                ("dbg_normhs", normhs[:]), ("dbg_s", s_tm[:]),
                                ("dbg_g", g_j[:]), ("dbg_hqfm", hq_fm[:]),
                                ("dbg_a", a4[:]), ("dbg_rstd", tiny[:, 0:8]),
                                ("dbg_rstd2", tiny[:, 12:20]),
                                ("dbg_hst", hs_t[:]), ("dbg_ut", u_t[:]),
                                ("dbg_gat", gat_t[:]), ("dbg_v4", v4[:]),
                                ("dbg_carry", carry[:])]:
                    rec.op("sp", lambda e, nm=nm, src=src: e.dma_start(
                        out=dbg[nm], in_=src), inc=sOut, n=16)
                rec.op("sp", lambda e: e.dma_start(
                    out=dbg["dbg_waT"],
                    in_=waT_d.rearrange("d m q c -> d (m q c)")), inc=sOut, n=16)
                rec.op("sp", lambda e: e.dma_start(
                    out=dbg["dbg_wgT"], in_=wgT_d), inc=sOut, n=16)
                rec.op("sp", lambda e: e.dma_start(
                    out=dbg["dbg_wdT"], in_=wdT_d), inc=sOut, n=16)
                rec.wait("sp", sOut, rec.val(sOut))
                for eng in ("dve", "act", "pe", "pool"):
                    rec.wait(eng, sOut, rec.val(sOut))

            prev_pe_end = rec.val(sPE)
            prev_dve_end = rec.val(sDV)
            prev_act_end = rec.val(sAC)
            prev_po_end = carr_all - KD

        rec.wait("sp", sOut, rec.val(sOut))

        block.sync(rec.replay_body("sp"))
        block.tensor(rec.replay_body("pe"))
        block.vector(rec.replay_body("dve"))
        block.scalar(rec.replay_body("act"))
        block.gpsimd(rec.replay_body("pool"))

    return nc


def kernel(L0_attn_x, L0_attn_W_in, L0_attn_W_ig, L0_attn_b_ig, L0_attn_W_rg,
           L0_attn_b_rg, L0_attn_lam, L0_attn_W_out, L0_attn_h0, L0_attn_norm_w,
           L0_ffn_w_gate, L0_ffn_w_up, L0_ffn_w_down):
    x = np.ascontiguousarray(np.asarray(L0_attn_x, dtype=np.float32))
    ident = np.eye(128, dtype=ml_dtypes.bfloat16)
    ones = np.ones((128, 1), dtype=ml_dtypes.bfloat16)
    consts = np.tile(np.array(CONSTS, np.float32)[None, :], (128, 1))

    common = {
        "w_in": np.ascontiguousarray(np.asarray(L0_attn_W_in, np.float32)),
        "w_ig": np.ascontiguousarray(np.asarray(L0_attn_W_ig, np.float32)),
        "w_rg": np.ascontiguousarray(np.asarray(L0_attn_W_rg, np.float32)),
        "w_out": np.ascontiguousarray(np.asarray(L0_attn_W_out, np.float32)),
        "wg": np.ascontiguousarray(np.asarray(L0_ffn_w_gate, np.float32)),
        "wu": np.ascontiguousarray(np.asarray(L0_ffn_w_up, np.float32)),
        "wd": np.ascontiguousarray(np.asarray(L0_ffn_w_down, np.float32)),
        "b_ig": np.ascontiguousarray(np.asarray(L0_attn_b_ig, np.float32)),
        "b_rg": np.ascontiguousarray(np.asarray(L0_attn_b_rg, np.float32)),
        "lam": np.ascontiguousarray(np.asarray(L0_attn_lam, np.float32)),
        "norm_w": np.ascontiguousarray(np.asarray(L0_attn_norm_w, np.float32)),
        "h0": np.ascontiguousarray(np.asarray(L0_attn_h0, np.float32).reshape(D)),
        "ident": ident, "onescol": ones, "consts": consts,
    }
    in_maps = []
    for c in range(NCORE):
        m = dict(common)
        m["x"] = np.ascontiguousarray(x[c])
        in_maps.append(m)

    nc = build_nc(debug=DEBUG)
    res = run_bass_kernel_spmd(nc, in_maps, core_ids=list(range(NCORE)))
    out = np.stack([res.results[c]["out"] for c in range(NCORE)], axis=0)
    return out
